# version 05
# baseline (speedup 1.0000x reference)
import contextlib
import math
import numpy as np

import concourse.bass as bass
import concourse.mybir as mybir
from concourse.bass_utils import run_bass_kernel_spmd

F32 = mybir.dt.float32
BF16 = mybir.dt.bfloat16
AF = mybir.ActivationFunctionType
ALU = mybir.AluOpType
AX = mybir.AxisListType.X

B, T, E = 2, 8192, 1024
H, DK, DV = 8, 128, 256
NCORE = 8
NCHUNK = T // 128
NGRP = T // 512
NST = 16
TOWN = 2048
EPS = 1e-5
NEX = 8

ENGS = ("sync", "gpsimd", "scalar", "vector", "tensor")


def _gammas():
    lo, hi = math.log(1.0 / 32.0), math.log(1.0 / 512.0)
    return 1.0 - np.exp(np.linspace(lo, hi, H))


def _perm_cols(h):
    return [h * DK + d for d in range(0, DK, 2)] + [h * DK + d for d in range(1, DK, 2)]


def _tables_for(heads):
    D = _gammas()
    t = np.arange(T, dtype=np.float64)
    out = [np.empty((T, 128), np.float64) for _ in range(4)]
    for hl, h in enumerate(heads):
        m = np.arange(64, dtype=np.float64)
        j = h * 64 + m
        invf = 10000.0 ** (-j / 512.0)
        sv = (2.0 * j + 0.4 * E) / (1.4 * E)
        ang = t[:, None] * invf[None, :]
        c, s = np.cos(ang), np.sin(ang)
        scq = sv[None, :] ** (t[:, None] / 512.0)
        sck = sv[None, :] ** (-t[:, None] / 512.0)
        Dh = float(D[h])
        a = Dh ** (t % 512) / np.sqrt((1.0 - Dh ** (t + 1)) / (1.0 - Dh))
        bk = Dh ** (-(t % 128)) / math.sqrt(DK)
        sl = slice(hl * 64, hl * 64 + 64)
        out[0][:, sl] = c * scq * a[:, None]
        out[1][:, sl] = s * scq * a[:, None]
        out[2][:, sl] = c * sck * bk[:, None]
        out[3][:, sl] = s * sck * bk[:, None]
    return [x.astype(np.float32) for x in out]


def _s_tab_for(heads):
    D = _gammas()
    vals = np.zeros(1088, np.float64)
    for hl, h in enumerate(heads):
        for st in range(NST):
            base = hl * 544 + 2 * st * (st + 1)
            for jb in range(4 * (st + 1)):
                vals[base + jb] = math.exp(math.log(float(D[h])) * (st * 512 - jb * 128))
    v32 = vals.astype(np.float32)
    return np.ascontiguousarray(np.broadcast_to(v32[None, :], (128, 1088)))


def _stage_core(c, x, wqkv, bqkv, wg, bg, wp, bp, gw, ew, eb, cw):
    b, q = c // 4, c % 4
    heads = (2 * q, 2 * q + 1)
    h0, h1 = heads
    qcols = _perm_cols(h0) + _perm_cols(h1)
    kcols = [E + d for d in qcols]
    vcols = list(range(2 * E + h0 * DV, 2 * E + (h0 + 1) * DV)) + \
            list(range(2 * E + h1 * DV, 2 * E + (h1 + 1) * DV))
    gcols = list(range(h0 * DV, (h0 + 1) * DV)) + list(range(h1 * DV, (h1 + 1) * DV))
    wqkvg = np.concatenate(
        [wqkv[:, qcols], wqkv[:, kcols], wqkv[:, vcols], wg[:, gcols]], axis=1)
    bvec = np.concatenate([bqkv[qcols], bqkv[kcols], bqkv[vcols], bg[gcols]])
    qC, qS, kC, kS = _tables_for(heads)
    wp_rows = [d - 2 * E for d in vcols]
    return {
        "x_full": np.ascontiguousarray(x[b]),
        "x_own": np.ascontiguousarray(x[b, q * TOWN:(q + 1) * TOWN]),
        "wqkvg": np.ascontiguousarray(wqkvg),
        "bias_bc": np.ascontiguousarray(np.broadcast_to(bvec[None, :], (128, 1536))),
        "qtabC": qC, "qtabS": qS, "ktabC": kC, "ktabS": kS,
        "s_tab": _s_tab_for(heads),
        "wp_slice": np.ascontiguousarray(wp[wp_rows, :]),
        "bp_bc": np.ascontiguousarray(np.broadcast_to(bp[None, :], (128, E))),
        "gwT": np.ascontiguousarray(gw.T),
        "ewT": np.ascontiguousarray(ew.transpose(0, 2, 1)),
        "eb_r": np.ascontiguousarray(
            eb.reshape(NEX, 8, 128).transpose(2, 0, 1).reshape(128, 64)),
        "cwT": np.ascontiguousarray(cw.T),
    }


class Em:

    def __init__(self, which, handle, sems):
        self.which = which
        self.h = handle
        self.sems = sems
        self.cnt = {k: 0 for k in sems}

    def op(self, eng, fn, waits=(), inc=None, incn=1):
        if self.which == eng:
            for s, v in waits:
                val = self.cnt[s] if v is None else v
                if val > 0:
                    self.h.wait_ge(self.sems[s], val)
            inst = fn(self.h)
            if inc is not None:
                assert inst is not None, f"op for {inc} returned no instruction"
                inst.then_inc(self.sems[inc], incn)
        if inc is not None:
            self.cnt[inc] += incn

    def wait(self, eng, sem, v=None):
        if self.which == eng:
            val = self.cnt[sem] if v is None else v
            if val > 0:
                self.h.wait_ge(self.sems[sem], val)


def _ln_stats_ops(st, so, xc, sq):
    def v_sq(e):
        return e.tensor_tensor(sq[:], xc, xc, ALU.mult)

    def v_red(e):
        e.reduce_sum(st[:, so:so + 1], xc, axis=AX, negate=True)
        return e.reduce_sum(st[:, so + 1:so + 2], sq[:], axis=AX)

    def s_stats(e):
        e.mul(st[:, so + 4:so + 5], st[:, so:so + 1], 1.0 / 1024.0)
        e.activation(st[:, so + 5:so + 6], st[:, so + 1:so + 2],
                     AF.Identity, scale=1.0 / 1024.0, bias=np.float32(EPS))
        e.activation(st[:, so + 6:so + 7], st[:, so + 4:so + 5], AF.Square)
        e.copy(st[:, so + 9:so + 10], st[:, so + 5:so + 6])
        return e.activation(st[:, so + 7:so + 8], st[:, so + 6:so + 7],
                            AF.Identity, scale=-1.0, bias=st[:, so + 5:so + 6])

    def v_recip(e):
        return e.reciprocal(st[:, so + 2:so + 3], st[:, so + 7:so + 8])
    return v_sq, v_red, s_stats, v_recip


def _lncol(sub, ec):
    return sub * 1024 + (ec * 128 if ec < 4 else 512 + (ec - 4) * 128)


def build_nc(dump=(), phases=5):
    nc = bass.Bass("TRN2", target_bir_lowering=False, num_devices=NCORE)

    ap_in = {}
    for name, shape in [
        ("x_full", [T, E]), ("x_own", [TOWN, E]), ("wqkvg", [E, 1536]),
        ("bias_bc", [128, 1536]), ("qtabC", [T, 128]), ("qtabS", [T, 128]),
        ("ktabC", [T, 128]), ("ktabS", [T, 128]), ("s_tab", [128, 1088]),
        ("wp_slice", [512, E]), ("bp_bc", [128, E]), ("gwT", [E, 8]),
        ("ewT", [NEX, E, E]), ("eb_r", [128, 64]), ("cwT", [E, E]),
    ]:
        ap_in[name] = nc.dram_tensor(name, shape, F32, kind="ExternalInput").ap()
    y = nc.dram_tensor("y", [TOWN, E], F32, kind="ExternalOutput").ap()

    KS = nc.dram_tensor("KS", [256, T], BF16, kind="Internal").ap()
    QS = nc.dram_tensor("QS", [256, T], BF16, kind="Internal").ap()
    VS = nc.dram_tensor("VS", [T, 512], BF16, kind="Internal").ap()
    GS = nc.dram_tensor("GS", [T, 512], BF16, kind="Internal").ap()
    PP = nc.dram_tensor("PP", [T, E], F32, kind="Internal").ap()
    PR = nc.dram_tensor("PR", [TOWN, E], F32, kind="Internal").ap()
    X2S = nc.dram_tensor("X2S", [TOWN, E], F32, kind="Internal").ap()
    LNS = nc.dram_tensor("LNS", [T, E], BF16, kind="Internal").ap()
    VS2 = nc.dram_tensor("VS2", [T, 512], BF16, kind="Internal").ap()
    HS2 = nc.dram_tensor("HS2", [512, T], BF16, kind="Internal").ap()
    OS2 = nc.dram_tensor("OS2", [T, 512], F32, kind="Internal").ap()

    idb = nc.inline_tensor(np.eye(128, dtype=np.float32), name="idb")
    ones_h = nc.inline_tensor(np.ones((1, 128), np.float32), name="ones_h")
    tri_h = []
    jl = np.arange(128)[:, None]
    il = np.arange(512)[None, :]
    for o in range(4):
        tri_h.append(nc.inline_tensor(
            (il >= jl + o * 128).astype(np.float32), name=f"tri{o}"))

    def chunks(apx, p=128):
        return apx.rearrange("(c p) n -> p c n", p=p)

    with contextlib.ExitStack() as _st:
        wsb = _st.enter_context(nc.sbuf_tensor("wsb", [128, 8 * 1536], BF16))
        bias_sb = _st.enter_context(nc.sbuf_tensor("bias_sb", [128, 1536], F32))
        idb_sb = _st.enter_context(nc.sbuf_tensor("idb_sb", [128, 128], BF16))
        x_in = _st.enter_context(nc.sbuf_tensor("x_in", [128, 2 * 1024], F32))
        xsq = _st.enter_context(nc.sbuf_tensor("xsq", [128, 1024], F32))
        t1sb = _st.enter_context(nc.sbuf_tensor("t1sb", [128, 2 * 1024], F32))
        ln_sb = _st.enter_context(nc.sbuf_tensor("ln_sb", [128, 2 * 1024], BF16))
        lnT_sb = _st.enter_context(nc.sbuf_tensor("lnT_sb", [128, 4096], BF16))
        stats = _st.enter_context(nc.sbuf_tensor("stats", [128, 2 * 16], F32))
        tabs = _st.enter_context(nc.sbuf_tensor("tabs", [128, 2 * 4 * 512], F32))
        qkw = _st.enter_context(nc.sbuf_tensor("qkw", [128, 2 * 512], F32))
        xw = _st.enter_context(nc.sbuf_tensor("xw", [128, 1024], F32))
        qkt = _st.enter_context(nc.sbuf_tensor("qkt", [128, 2 * 512], BF16))
        gwk = _st.enter_context(nc.sbuf_tensor("gwk", [128, 2 * 512], F32))
        vstage = _st.enter_context(nc.sbuf_tensor("vstage", [128, 2048], BF16))
        gstage = _st.enter_context(nc.sbuf_tensor("gstage", [128, 2048], BF16))
        qstage = _st.enter_context(nc.sbuf_tensor("qstage", [128, 1024], BF16))
        kstage = _st.enter_context(nc.sbuf_tensor("kstage", [128, 1024], BF16))
        ps_lnT = _st.enter_context(nc.psum_tensor("ps_lnT", [128, 1024], F32))
        ps_qkv = _st.enter_context(nc.psum_tensor("ps_qkv", [128, 2 * 512], F32))
        ps_tr = _st.enter_context(nc.psum_tensor("ps_tr", [128, 2048], F32))
        s_names = ["sx", "stab", "swld", "sa", "sb", "sln", "slntr", "slnev",
                   "sqkv", "sqkev", "ssilu", "sxpos", "sqktr", "strev",
                   "swout", "sxfree", "sdbg", "sc", "sxp1"]
        if True:
            stack = _st
            sems = {n: stack.enter_context(nc.semaphore(n)) for n in s_names}
            blk = stack.enter_context(nc.Block())

            def phase1(em: Em):
                xv = chunks(ap_in["x_full"])
                tabins = [ap_in["qtabC"], ap_in["qtabS"], ap_in["ktabC"], ap_in["ktabS"]]
                tb_all = tabs.ap().rearrange("p (b k s n) -> p b k s n", b=2, k=4, s=4)

                em.op("gpsimd", lambda e: e.dma_start(
                    wsb.ap().rearrange("p (c n) -> p c n", c=8),
                    chunks(ap_in["wqkvg"])), inc="swld", incn=16)
                em.op("sync", lambda e: e.dma_start(bias_sb[:], ap_in["bias_bc"][:]),
                      inc="swld", incn=16)
                em.op("gpsimd", lambda e: e.dma_start(idb_sb[:], idb.ap()[:]),
                      inc="swld", incn=16)
                for t0 in range(2):
                    em.op("sync", lambda e, t0=t0: e.dma_start(
                        x_in[:, t0 * 1024:(t0 + 1) * 1024], xv[:, t0, :]),
                        inc="sx", incn=16)
                for ti in range(4):
                    em.op("sync", lambda e, ti=ti: e.dma_start(
                        tb_all[:, 0, ti], chunks(tabins[ti])[:, 0:4, :]),
                        inc="stab", incn=16)

                for g in range(NGRP):
                    gb = g % 2
                    if g + 1 < NGRP:
                        for ti in range(4):
                            em.op("sync", lambda e, ti=ti, g=g: e.dma_start(
                                tb_all[:, (g + 1) % 2, ti],
                                chunks(tabins[ti])[:, 4 * (g + 1):4 * (g + 2), :]),
                                waits=[("sxpos", max(0, 4 * g))],
                                inc="stab", incn=16)
                    for sub in range(4):
                        tc = 4 * g + sub
                        xb = tc % 2
                        xoff = xb * 1024
                        so = xb * 16
                        if tc + 2 < NCHUNK:
                            em.op("sync", lambda e, tc=tc, xb=xb: e.dma_start(
                                x_in[:, xb * 1024:(xb + 1) * 1024], xv[:, tc + 2, :]),
                                waits=[("sxfree", tc + 1)], inc="sx", incn=16)
                        v_sq, v_red, s_stats, v_recip = _ln_stats_ops(stats, so, x_in[:, xoff:xoff + 1024], xsq)
                        em.op("vector", v_sq,
                              waits=[("sx", 16 * (tc + 1)), ("sln", max(0, tc - 1))],
                              inc="sa")
                        em.op("vector", v_red, waits=[("sa", None)], inc="sb")
                        em.op("scalar", s_stats, waits=[("sb", None)], inc="sc")
                        em.op("scalar", lambda e, xoff=xoff, so=so: e.activation(
                            t1sb[:, xoff:xoff + 1024], x_in[:, xoff:xoff + 1024],
                            AF.Identity, bias=stats[:, so + 4:so + 5]),
                            inc="sxfree")
                        em.op("vector", v_recip, waits=[("sc", None)], inc="sb")
                        def s_ln(e, xoff=xoff, so=so):
                            e.sqrt(stats[:, so + 8:so + 9], stats[:, so + 2:so + 3])
                            e.copy(stats[:, so + 9:so + 10], stats[:, so + 8:so + 9])
                            return e.activation(
                                ln_sb[:, xoff:xoff + 1024], t1sb[:, xoff:xoff + 1024],
                                AF.Identity, scale=stats[:, so + 8:so + 9])
                        em.op("scalar", s_ln,
                              waits=[("sb", None), ("slntr", max(0, tc - 1))], inc="sln")

                        def lntr(e, xoff=xoff):
                            for ec in range(8):
                                col = (0 if ec < 4 else 512) + (ec % 4) * 128
                                inst = e.matmul(
                                    ps_lnT[:, col:col + 128],
                                    ln_sb[:, xoff + ec * 128:xoff + (ec + 1) * 128],
                                    idb_sb[:], start=(ec % 4 == 0), stop=True)
                            return inst
                        wl = [("sln", None), ("slnev", em.cnt["slnev"])]
                        if tc == 0:
                            wl.append(("swld", 48))
                        em.op("tensor", lntr, waits=wl, inc="slntr")
                        em.op("gpsimd", lambda e, tc=tc, xoff=xoff: e.dma_start(
                            LNS.rearrange("(c p) n -> p c n", p=128)[:, tc, :],
                            ln_sb[:, xoff:xoff + 1024]),
                            waits=[("sln", None)], inc="sdbg", incn=16)

                        def lnev(e, sub=sub):
                            e.tensor_copy(lnT_sb[:, sub * 1024:sub * 1024 + 512],
                                          ps_lnT[:, 0:512])
                            return e.tensor_copy(
                                lnT_sb[:, sub * 1024 + 512:sub * 1024 + 1024],
                                ps_lnT[:, 512:1024])
                        em.op("vector", lnev,
                              waits=[("slntr", None), ("sqkv", 12 * max(0, g))],
                              inc="slnev")

                    for sub in range(4):
                        tc = 4 * g + sub
                        for grp in range(3):
                            gi = em.cnt["sqkv"]

                            def qkvmm(e, sub=sub, grp=grp, gi=gi):
                                pb = (gi % 2) * 512
                                for ec in range(8):
                                    inst = e.matmul(
                                        ps_qkv[:, pb:pb + 512],
                                        lnT_sb[:, _lncol(sub, ec):_lncol(sub, ec) + 128],
                                        wsb[:, ec * 1536 + grp * 512:ec * 1536 + (grp + 1) * 512],
                                        start=(ec == 0), stop=(ec == 7))
                                return inst
                            em.op("tensor", qkvmm,
                                  waits=[("slnev", 4 * g + sub + 1),
                                         ("sqkev", max(0, gi - 1))],
                                  inc="sqkv")

                            def qkev(e, sub=sub, grp=grp, gi=gi):
                                pb = (gi % 2) * 512
                                bb = bias_sb[:, grp * 512:(grp + 1) * 512]
                                if grp == 0:
                                    out = qkw[:, (sub % 2) * 512:((sub % 2) + 1) * 512]
                                elif grp == 1:
                                    out = vstage[:, sub * 512:(sub + 1) * 512]
                                else:
                                    out = gwk[:, (sub % 2) * 512:((sub % 2) + 1) * 512]
                                return e.tensor_tensor(out, ps_qkv[:, pb:pb + 512], bb, ALU.add)
                            wv = [("sqkv", None)]
                            if g == 0 and sub == 0 and grp == 0:
                                wv.append(("swld", 48))
                            if grp == 1 and sub == 0:
                                wv.append(("swout", 64 * g))
                            if grp == 2:
                                wv.append(("ssilu", max(0, tc - 1)))
                            em.op("vector", qkev, waits=wv, inc="sqkev")
                        em.op("scalar", lambda e, sub=sub: e.activation(
                            gstage[:, sub * 512:(sub + 1) * 512],
                            gwk[:, (sub % 2) * 512:((sub % 2) + 1) * 512], AF.Silu),
                            waits=[("sqkev", None)], inc="ssilu")

                        def xpos_prod(e, sub=sub, gb=gb):
                            qk4 = qkw.ap().rearrange("p (w b h x) -> p w b h x", w=2, b=4, h=2)
                            xw8 = xw.ap().rearrange("p (b h x) -> p b h x", b=8, h=2)
                            wv_ = sub % 2
                            for part in range(2):
                                C = tb_all[:, gb, 2 * part, sub].rearrange("p (h x) -> p h x", h=2)
                                S = tb_all[:, gb, 2 * part + 1, sub].rearrange("p (h x) -> p h x", h=2)
                                qE = qk4[:, wv_, 2 * part:2 * part + 2, 0]
                                qO = qk4[:, wv_, 2 * part:2 * part + 2, 1]
                                o4 = 4 * part
                                e.tensor_tensor(xw8[:, o4 + 0:o4 + 2, 0], qE, C, ALU.mult)
                                e.tensor_tensor(xw8[:, o4 + 0:o4 + 2, 1], qO, C, ALU.mult)
                                e.tensor_tensor(xw8[:, o4 + 2:o4 + 4, 0], qO, S, ALU.mult)
                                inst = e.tensor_tensor(xw8[:, o4 + 2:o4 + 4, 1], qE, S, ALU.mult)
                            return inst
                        em.op("vector", xpos_prod,
                              waits=[("stab", 16 * 4 * (g + 1)), ("sqktr", max(0, tc - 1)),
                                     ("sqkev", None)],
                              inc="sxp1")
                        def xpos_comb(e, sub=sub):
                            ot4 = qkt.ap().rearrange("p (w b h x) -> p w b h x", w=2, b=4, h=2)
                            xw8 = xw.ap().rearrange("p (b h x) -> p b h x", b=8, h=2)
                            wv_ = sub % 2
                            for part in range(2):
                                o4 = 4 * part
                                e.tensor_tensor(ot4[:, wv_, 2 * part:2 * part + 2, 0],
                                                xw8[:, o4 + 0:o4 + 2, 0], xw8[:, o4 + 2:o4 + 4, 0], ALU.subtract)
                                inst = e.tensor_tensor(ot4[:, wv_, 2 * part:2 * part + 2, 1],
                                                       xw8[:, o4 + 0:o4 + 2, 1], xw8[:, o4 + 2:o4 + 4, 1], ALU.add)
                            return inst
                        em.op("vector", xpos_comb, waits=[("sxp1", None)], inc="sxpos")

                        def qktr(e, sub=sub):
                            wv_ = sub % 2
                            for cc in range(4):
                                inst = e.matmul(
                                    ps_tr[:, cc * 512 + sub * 128:cc * 512 + (sub + 1) * 128],
                                    qkt[:, wv_ * 512 + cc * 128:wv_ * 512 + (cc + 1) * 128],
                                    idb_sb[:], start=(sub == 0), stop=True)
                            return inst
                        em.op("tensor", qktr,
                              waits=[("sxpos", None), ("strev", em.cnt["strev"])],
                              inc="sqktr")

                    def trev(e):
                        e.tensor_copy(qstage[:, 0:512], ps_tr[:, 0:512])
                        e.tensor_copy(qstage[:, 512:1024], ps_tr[:, 512:1024])
                        e.tensor_copy(kstage[:, 0:512], ps_tr[:, 1024:1536])
                        return e.tensor_copy(kstage[:, 512:1024], ps_tr[:, 1536:2048])
                    em.op("vector", trev,
                          waits=[("sqktr", None), ("swout", 64 * g)], inc="strev")
                    for dst, src in [(QS, qstage), (KS, kstage)]:
                        em.op("gpsimd", lambda e, dst=dst, src=src, g=g: e.dma_start(
                            dst.rearrange("(c p) t -> p c t", p=128)[:, :, g * 512:(g + 1) * 512],
                            src.ap().rearrange("p (c t) -> p c t", c=2)),
                            waits=[("strev", None)], inc="swout", incn=16)
                    em.op("gpsimd", lambda e, g=g: e.dma_start(
                        VS.rearrange("(c p) v -> p c v", p=128)[:, 4 * g:4 * g + 4, :],
                        vstage.ap().rearrange("p (c v) -> p c v", c=4)),
                        waits=[("sqkev", None)], inc="swout", incn=16)
                    em.op("gpsimd", lambda e, g=g: e.dma_start(
                        GS.rearrange("(c p) v -> p c v", p=128)[:, 4 * g:4 * g + 4, :],
                        gstage.ap().rearrange("p (c v) -> p c v", c=4)),
                        waits=[("ssilu", None)], inc="swout", incn=16)
                em.wait("gpsimd", "swout", 64 * NGRP)

            for which in ENGS:
                getattr(blk, which)(
                    (lambda w: lambda h: phase1(Em(w, h, sems)))(which))

    nc.all_engine_barrier()

    if phases < 2:
        _emit_dumps(nc, dump, dict(KS=KS, QS=QS, VS=VS, GS=GS, PP=PP, PR=PR, X2S=X2S, LNS=LNS, VS2=VS2, HS2=HS2, OS2=OS2))
        return nc
    with contextlib.ExitStack() as _st:
        K2 = _st.enter_context(nc.sbuf_tensor("K2", [128, 2 * T], BF16))
        Q2 = _st.enter_context(nc.sbuf_tensor("Q2", [128, 2 * T], BF16))
        V2 = _st.enter_context(nc.sbuf_tensor("V2", [128, 2 * 64 * 256], BF16))
        G2 = _st.enter_context(nc.sbuf_tensor("G2", [128, 2 * 1024], BF16))
        ssb = _st.enter_context(nc.sbuf_tensor("ssb", [128, 1088], F32))
        tri_sb = _st.enter_context(nc.sbuf_tensor("tri_sb", [128, 4 * 512], BF16))
        idb2 = _st.enter_context(nc.sbuf_tensor("idb2", [128, 128], BF16))
        wp_sb = _st.enter_context(nc.sbuf_tensor("wp_sb", [128, 4 * 1024], BF16))
        stsb = _st.enter_context(nc.sbuf_tensor("stsb", [128, 2 * 512], BF16))
        osb = _st.enter_context(nc.sbuf_tensor("osb", [128, 2 * 256], F32))
        osq = _st.enter_context(nc.sbuf_tensor("osq", [128, 256], F32))
        gt_sb = _st.enter_context(nc.sbuf_tensor("gt_sb", [128, 2 * 256], BF16))
        ht_tok = _st.enter_context(nc.sbuf_tensor("ht_tok", [128, 2 * 256], BF16))
        hT_sb = _st.enter_context(nc.sbuf_tensor("hT_sb", [128, 4 * 512], BF16))
        st2 = _st.enter_context(nc.sbuf_tensor("st2", [128, 2 * 16], F32))
        ppst = _st.enter_context(nc.sbuf_tensor("ppst", [128, 4096], F32))
        ps_st = _st.enter_context(nc.psum_tensor("ps_st", [128, 2 * 512], F32))
        ps_O = _st.enter_context(nc.psum_tensor("ps_O", [128, 1024], F32))
        ps_h = _st.enter_context(nc.psum_tensor("ps_h", [128, 1024], F32))
        ps_pp = _st.enter_context(nc.psum_tensor("ps_pp", [128, 2 * 512], F32))
        s2_names = ["p2ld", "sgld", "sstmm", "sstev", "sdiag", "somm", "sgn1",
                    "sgn2", "sosub", "sgt", "sht", "shttr", "shev", "soev",
                    "sppmm", "sppev", "sppwr", "scc", "sq1", "sgn2a", "sgn2b", "sdbg2", "sgn1f"]
        if True:
            stack = _st
            sems2 = {n: stack.enter_context(nc.semaphore(n)) for n in s2_names}
            blk2 = stack.enter_context(nc.Block())

            def phase2(em: Em):
                for hl in range(2):
                    em.op("sync", lambda e, hl=hl: e.dma_start(
                        K2[:, hl * T:(hl + 1) * T],
                        KS.rearrange("(c p) t -> p c t", p=128)[:, hl, :]),
                        inc="p2ld", incn=16)
                    em.op("sync", lambda e, hl=hl: e.dma_start(
                        Q2[:, hl * T:(hl + 1) * T],
                        QS.rearrange("(c p) t -> p c t", p=128)[:, hl, :]),
                        inc="p2ld", incn=16)
                    for part in range(8):
                        em.op("sync", lambda e, hl=hl, part=part: e.dma_start(
                            V2.ap().rearrange("p (h c v) -> p h c v", h=2, c=64)[:, hl, part * 8:(part + 1) * 8],
                            VS.rearrange("(c p) v -> p c v", p=128)[:, part * 8:(part + 1) * 8, hl * 256:(hl + 1) * 256]),
                            inc="p2ld", incn=16)
                em.op("sync", lambda e: e.dma_start(ssb[:], ap_in["s_tab"][:]),
                      inc="p2ld", incn=16)
                for o in range(4):
                    em.op("gpsimd", lambda e, o=o: e.dma_start(
                        tri_sb[:, o * 512:(o + 1) * 512], tri_h[o].ap()[:]),
                        inc="p2ld", incn=16)
                em.op("gpsimd", lambda e: e.dma_start(idb2[:], idb.ap()[:]),
                      inc="p2ld", incn=16)
                em.op("gpsimd", lambda e: e.dma_start(
                    wp_sb.ap().rearrange("p (c n) -> p c n", c=4),
                    ap_in["wp_slice"].rearrange("(c p) n -> p c n", p=128)),
                    inc="p2ld", incn=16)
                ldtot = em.cnt["p2ld"]
                for hl_ in range(2):
                    em.op("gpsimd", lambda e, hl_=hl_: e.dma_start(
                        VS2.rearrange("(c p) v -> p c v", p=128)[:, :, hl_ * 256:(hl_ + 1) * 256],
                        V2.ap().rearrange("p (h c v) -> p h c v", h=2, c=64)[:, hl_]),
                        waits=[("p2ld", ldtot)], inc="sdbg2", incn=16)

                jdx = 0
                for st in range(NST):
                    for hl in range(2):
                        n = 2 * st + hl
                        em.op("sync", lambda e, st=st, hl=hl: e.dma_start(
                            G2[:, hl * 1024:(hl + 1) * 1024].rearrange("p (c v) -> p c v", c=4),
                            GS.rearrange("(c p) v -> p c v", p=128)[:, 4 * st:4 * st + 4, hl * 256:(hl + 1) * 256]),
                            waits=[("sgt", max(0, 4 * (n - 1)))],
                            inc="sgld", incn=16)
                        nj = 4 * (st + 1)
                        for jb in range(nj):
                            sidx = hl * 544 + 2 * st * (st + 1) + jb
                            diag = jb >= 4 * st

                            def stmm(e, st=st, hl=hl, jb=jb, jdx=jdx):
                                pb = (jdx % 2) * 512
                                return e.matmul(
                                    ps_st[:, pb:pb + 512],
                                    K2[:, hl * T + jb * 128:hl * T + (jb + 1) * 128],
                                    Q2[:, hl * T + st * 512:hl * T + (st + 1) * 512],
                                    start=True, stop=True)
                            wt = [("sstev", max(0, jdx - 1)), ("somm", jdx)]
                            if jdx == 0:
                                wt.append(("p2ld", ldtot))
                            if jb == 0:
                                wt.append(("soev", n))
                            em.op("tensor", stmm, waits=wt, inc="sstmm")

                            ws = [("sstmm", None), ("somm", max(0, jdx - 1))]
                            if jdx == 0:
                                ws.append(("p2ld", ldtot))
                            em.op("scalar", lambda e, jdx=jdx, sidx=sidx: e.mul(
                                stsb[:, (jdx % 2) * 512:((jdx % 2) + 1) * 512],
                                ps_st[:, (jdx % 2) * 512:((jdx % 2) + 1) * 512],
                                ssb[:, sidx:sidx + 1]),
                                waits=ws, inc="sstev")
                            if diag:
                                o = jb - 4 * st
                                wd = [("sstev", None)]
                                if em.cnt["sdiag"] == 0:
                                    wd.append(("p2ld", ldtot))
                                em.op("vector", lambda e, jdx=jdx, o=o: e.tensor_tensor(
                                    stsb[:, (jdx % 2) * 512:((jdx % 2) + 1) * 512],
                                    stsb[:, (jdx % 2) * 512:((jdx % 2) + 1) * 512],
                                    tri_sb[:, o * 512:(o + 1) * 512], ALU.mult),
                                    waits=wd, inc="sdiag")

                            def omm(e, hl=hl, jb=jb, jdx=jdx,
                                    first=(jb == 0), last=(jb == nj - 1)):
                                pb = (jdx % 2) * 512
                                for sub in range(4):
                                    inst = e.matmul(
                                        ps_O[:, sub * 256:(sub + 1) * 256],
                                        stsb[:, pb + sub * 128:pb + (sub + 1) * 128],
                                        V2[:, hl * 16384 + jb * 256:hl * 16384 + (jb + 1) * 256],
                                        start=(first and sub % 2 == 0), stop=last)
                                return inst
                            wo = [("sstev", None)]
                            if diag:
                                wo.append(("sdiag", None))
                            em.op("tensor", omm, waits=wo, inc="somm")
                            jdx += 1
                        for sub in range(4):
                            so = (sub % 2) * 16
                            ob = (sub % 2) * 256
                            em.op("vector", lambda e, sub=sub, so=so: e.reduce_sum(
                                st2[:, so:so + 1], ps_O[:, sub * 256:(sub + 1) * 256],
                                axis=AX, negate=True),
                                waits=[("somm", None), ("sgt", max(0, em.cnt["sgt"] - 1))],
                                inc="sgn1")
                            em.op("vector", lambda e: e.nop(),
                                  waits=[("sgn1", None)], inc="sgn1f")
                            def s_osub(e, sub=sub, so=so, ob=ob):
                                e.mul(st2[:, so + 4:so + 5], st2[:, so:so + 1], 1.0 / 256.0)
                                e.copy(st2[:, so + 9:so + 10], st2[:, so + 4:so + 5])
                                return e.activation(osb[:, ob:ob + 256], ps_O[:, sub * 256:(sub + 1) * 256],
                                                    AF.Identity, bias=st2[:, so + 4:so + 5])
                            em.op("scalar", s_osub,
                                  waits=[("sgn1f", None), ("sht", max(0, em.cnt["sht"] - 1)),
                                         ("sdbg2", max(0, em.cnt["sdbg2"] - 16))],
                                  inc="sosub")
                            em.op("vector", lambda e, ob=ob: e.tensor_tensor(
                                osq[:], osb[:, ob:ob + 256], osb[:, ob:ob + 256], ALU.mult),
                                waits=[("sosub", None)], inc="sq1")
                            em.op("vector", lambda e, so=so: e.reduce_sum(
                                st2[:, so + 1:so + 2], osq[:], axis=AX),
                                waits=[("sq1", None)], inc="sgn2a")
                            def s_var(e, so=so):
                                e.activation(st2[:, so + 5:so + 6], st2[:, so + 1:so + 2],
                                             AF.Identity, scale=1.0 / 256.0, bias=np.float32(EPS))
                                e.activation(st2[:, so + 6:so + 7], st2[:, so + 4:so + 5], AF.Square)
                                e.copy(st2[:, so + 9:so + 10], st2[:, so + 5:so + 6])
                                return e.activation(st2[:, so + 7:so + 8], st2[:, so + 6:so + 7],
                                                    AF.Identity, scale=-1.0, bias=st2[:, so + 5:so + 6])
                            em.op("scalar", s_var, waits=[("sgn2a", None)], inc="sgn2b")
                            em.op("vector", lambda e, so=so: e.reciprocal(
                                st2[:, so + 2:so + 3], st2[:, so + 7:so + 8]),
                                waits=[("sgn2b", None)], inc="sgn2")
                            def gtile(e, sub=sub, hl=hl, so=so, ob=ob):
                                e.sqrt(st2[:, so + 8:so + 9], st2[:, so + 2:so + 3])
                                e.copy(st2[:, so + 10:so + 11], st2[:, so + 8:so + 9])
                                return e.activation(
                                    gt_sb[:, ob:ob + 256],
                                    G2[:, hl * 1024 + sub * 256:hl * 1024 + (sub + 1) * 256],
                                    AF.Identity, scale=st2[:, so + 8:so + 9])
                            em.op("scalar", gtile,
                                  waits=[("sgn2", None), ("sgld", 16 * (n + 1))],
                                  inc="sgt")
                            em.op("gpsimd", lambda e, st=st, hl=hl, sub=sub, ob=ob: e.dma_start(
                                OS2.rearrange("(c p) v -> p c v", p=128)[:, 4 * st + sub, hl * 256:(hl + 1) * 256],
                                osb[:, ob:ob + 256]),
                                waits=[("sosub", None)], inc="sdbg2", incn=16)
                            em.op("vector", lambda e, ob=ob: e.tensor_tensor(
                                ht_tok[:, ob:ob + 256], osb[:, ob:ob + 256],
                                gt_sb[:, ob:ob + 256], ALU.mult),
                                waits=[("sgt", None), ("shttr", max(0, em.cnt["shttr"] - 1))],
                                inc="sht")
                            def httr(e, sub=sub, ob=ob):
                                for vc in range(2):
                                    inst = e.matmul(
                                        ps_h[:, vc * 512 + sub * 128:vc * 512 + (sub + 1) * 128],
                                        ht_tok[:, ob + vc * 128:ob + (vc + 1) * 128],
                                        idb2[:], start=(sub == 0), stop=True)
                                return inst
                            wh = [("sht", None)]
                            if sub == 0:
                                wh.append(("shev", em.cnt["shev"]))
                            em.op("tensor", httr, waits=wh, inc="shttr")
                        em.op("scalar", lambda e: e.nop(), inc="soev")

                        def hev(e, hl=hl):
                            e.tensor_copy(hT_sb[:, hl * 1024:hl * 1024 + 512], ps_h[:, 0:512])
                            return e.tensor_copy(
                                hT_sb[:, hl * 1024 + 512:hl * 1024 + 1024], ps_h[:, 512:1024])
                        em.op("vector", hev,
                              waits=[("shttr", None), ("sppmm", 8 * st)],
                              inc="shev", incn=2)
                    for tc in range(4):
                        for half in range(2):
                            p = em.cnt["sppmm"]

                            def ppmm(e, tc=tc, half=half, p=p):
                                pb = (p % 2) * 512
                                for vcg in range(4):
                                    inst = e.matmul(
                                        ps_pp[:, pb:pb + 512],
                                        hT_sb[:, vcg * 512 + tc * 128:vcg * 512 + (tc + 1) * 128],
                                        wp_sb[:, vcg * 1024 + half * 512:vcg * 1024 + (half + 1) * 512],
                                        start=(vcg == 0), stop=(vcg == 3))
                                return inst
                            em.op("tensor", ppmm,
                                  waits=[("shev", None), ("sppev", max(0, p - 1))],
                                  inc="sppmm")
                            em.op("vector", lambda e, tc=tc, half=half, p=p: e.tensor_copy(
                                ppst[:, tc * 1024 + half * 512:tc * 1024 + (half + 1) * 512],
                                ps_pp[:, (p % 2) * 512:((p % 2) + 1) * 512]),
                                waits=[("sppmm", None), ("sppwr", 16 * st)],
                                inc="sppev")
                    em.op("sync", lambda e, st=st: e.dma_start(
                        PP.rearrange("(c p) e -> p c e", p=128)[:, 4 * st:4 * st + 4, :],
                        ppst.ap().rearrange("p (c e) -> p c e", c=4)),
                        waits=[("sppev", None)], inc="sppwr", incn=16)
                em.op("gpsimd", lambda e: e.collective_compute(
                    "ReduceScatter", ALU.add,
                    replica_groups=[[0, 1, 2, 3], [4, 5, 6, 7]],
                    ins=[PP], outs=[PR]),
                    waits=[("sppwr", 16 * NST)], inc="scc")
                em.wait("gpsimd", "scc", 1)
                em.wait("gpsimd", "sdbg2", None)

            for which in ENGS:
                getattr(blk2, which)(
                    (lambda w: lambda h: phase2(Em(w, h, sems2)))(which))

    nc.all_engine_barrier()

    if phases < 3:
        _emit_dumps(nc, dump, dict(KS=KS, QS=QS, VS=VS, GS=GS, PP=PP, PR=PR, X2S=X2S, LNS=LNS, VS2=VS2, HS2=HS2, OS2=OS2))
        return nc
    with contextlib.ExitStack() as _st:
        mT_sb = _st.enter_context(nc.sbuf_tensor("mT_sb", [1, 16384], BF16))
        acc = _st.enter_context(nc.sbuf_tensor("acc", [128, 16384], F32))
        _ln2t_stack = _st.enter_context(contextlib.ExitStack())
        ln2T = _ln2t_stack.enter_context(nc.sbuf_tensor("ln2T", [128, 16384], BF16))
        with contextlib.ExitStack() as _st:
            bp_sb = _st.enter_context(nc.sbuf_tensor("bp_sb", [128, 1024], F32))
            xo_in = _st.enter_context(nc.sbuf_tensor("xo_in", [128, 2 * 1024], F32))
            pr_in = _st.enter_context(nc.sbuf_tensor("pr_in", [128, 2 * 1024], F32))
            x2sb = _st.enter_context(nc.sbuf_tensor("x2sb", [128, 2 * 1024], F32))
            x2sq = _st.enter_context(nc.sbuf_tensor("x2sq", [128, 1024], F32))
            t2sb = _st.enter_context(nc.sbuf_tensor("t2sb", [128, 2 * 1024], F32))
            ln2c = _st.enter_context(nc.sbuf_tensor("ln2c", [128, 2 * 1024], BF16))
            st3 = _st.enter_context(nc.sbuf_tensor("st3", [128, 2 * 16], F32))
            gw_sb = _st.enter_context(nc.sbuf_tensor("gw_sb", [128, 64], BF16))
            idb3 = _st.enter_context(nc.sbuf_tensor("idb3", [128, 128], BF16))
            idf3 = _st.enter_context(nc.sbuf_tensor("idf3", [128, 128], F32))
            lgsb = _st.enter_context(nc.sbuf_tensor("lgsb", [128, 2 * 8], F32))
            msb = _st.enter_context(nc.sbuf_tensor("msb", [128, 2 * 8], F32))
            mwork = _st.enter_context(nc.sbuf_tensor("mwork", [128, 8], F32))
            ps3_lnT = _st.enter_context(nc.psum_tensor("ps3_lnT", [128, 1024], F32))
            ps3_lg = _st.enter_context(nc.psum_tensor("ps3_lg", [128, 8], F32))
            ps3_mt = _st.enter_context(nc.psum_tensor("ps3_mt", [1, 1024], F32))
            s3_names = ["s3ld", "s3x2", "s3xw", "s3a", "s3b", "s3ln", "s3tr",
                        "s3ev", "s3lg", "s3lgev", "s3sm", "s3ex", "s3m",
                        "s3mt", "s3mtev", "s3c", "s3tp"]
            if True:
                stack = _st
                sems3 = {n: stack.enter_context(nc.semaphore(n)) for n in s3_names}
                blk3 = stack.enter_context(nc.Block())

                def phase3(em: Em):
                    xov = chunks(ap_in["x_own"])
                    prv = PR.rearrange("(c p) e -> p c e", p=128)
                    em.op("sync", lambda e: e.dma_start(bp_sb[:], ap_in["bp_bc"][:]),
                          inc="s3ld", incn=16)
                    em.op("gpsimd", lambda e: e.dma_start(idb3[:], idb.ap()[:]),
                          inc="s3ld", incn=16)
                    em.op("sync", lambda e: e.dma_start(idf3[:], idb.ap()[:]),
                          inc="s3ld", incn=16)
                    em.op("gpsimd", lambda e: e.dma_start(
                        gw_sb.ap().rearrange("p (c n) -> p c n", c=8),
                        ap_in["gwT"].rearrange("(c p) n -> p c n", p=128)),
                        inc="s3ld", incn=16)
                    base = em.cnt["s3ld"]
                    for t0 in range(2):
                        em.op("sync", lambda e, t0=t0: e.dma_start(
                            xo_in[:, t0 * 1024:(t0 + 1) * 1024], xov[:, t0, :]),
                            inc="s3ld", incn=16)
                        em.op("sync", lambda e, t0=t0: e.dma_start(
                            pr_in[:, t0 * 1024:(t0 + 1) * 1024], prv[:, t0, :]),
                            inc="s3ld", incn=16)
                    for tc in range(16):
                        xb = tc % 2
                        xoff = xb * 1024
                        so = xb * 16
                        if tc + 2 < 16:
                            em.op("sync", lambda e, tc=tc, xb=xb: e.dma_start(
                                xo_in[:, xb * 1024:(xb + 1) * 1024], xov[:, tc + 2, :]),
                                waits=[("s3x2", tc + 1)], inc="s3ld", incn=16)
                            em.op("sync", lambda e, tc=tc, xb=xb: e.dma_start(
                                pr_in[:, xb * 1024:(xb + 1) * 1024], prv[:, tc + 2, :]),
                                waits=[("s3x2", tc + 1)], inc="s3ld", incn=16)

                        em.op("vector", lambda e, xoff=xoff: e.tensor_tensor(
                            x2sb[:, xoff:xoff + 1024], xo_in[:, xoff:xoff + 1024],
                            pr_in[:, xoff:xoff + 1024], ALU.add),
                            waits=[("s3ld", base + 32 * (tc + 1)),
                                   ("s3xw", 16 * max(0, tc - 1)),
                                   ("s3ln", max(0, tc - 1))],
                            inc="s3tp")
                        em.op("vector", lambda e, xoff=xoff: e.tensor_tensor(
                            x2sb[:, xoff:xoff + 1024], x2sb[:, xoff:xoff + 1024],
                            bp_sb[:], ALU.add),
                            waits=[("s3tp", None)], inc="s3x2")
                        em.op("gpsimd", lambda e, tc=tc, xoff=xoff: e.dma_start(
                            X2S.rearrange("(c p) e -> p c e", p=128)[:, tc, :],
                            x2sb[:, xoff:xoff + 1024]),
                            waits=[("s3x2", None)], inc="s3xw", incn=16)
                        v_sq, v_red, s_stats, v_recip = _ln_stats_ops(st3, so, x2sb[:, xoff:xoff + 1024], x2sq)
                        em.op("vector", v_sq, waits=[("s3x2", None)], inc="s3a")
                        em.op("vector", v_red, waits=[("s3a", None)], inc="s3b")
                        em.op("scalar", s_stats, waits=[("s3b", None)], inc="s3c")
                        em.op("scalar", lambda e, xoff=xoff, so=so: e.activation(
                            t2sb[:, xoff:xoff + 1024], x2sb[:, xoff:xoff + 1024],
                            AF.Identity, bias=st3[:, so + 4:so + 5]))
                        em.op("vector", v_recip, waits=[("s3c", None)], inc="s3b")
                        def s_ln2(e, xoff=xoff, so=so):
                            e.sqrt(st3[:, so + 8:so + 9], st3[:, so + 2:so + 3])
                            e.copy(st3[:, so + 9:so + 10], st3[:, so + 8:so + 9])
                            return e.activation(
                                ln2c[:, xoff:xoff + 1024], t2sb[:, xoff:xoff + 1024],
                                AF.Identity, scale=st3[:, so + 8:so + 9])
                        em.op("scalar", s_ln2,
                              waits=[("s3b", None), ("s3tr", max(0, tc - 1))], inc="s3ln")

                        def l2tr(e, xoff=xoff):
                            for ec in range(8):
                                col = (0 if ec < 4 else 512) + (ec % 4) * 128
                                inst = e.matmul(
                                    ps3_lnT[:, col:col + 128],
                                    ln2c[:, xoff + ec * 128:xoff + (ec + 1) * 128],
                                    idb3[:], start=(ec % 4 == 0), stop=True)
                            return inst
                        wl = [("s3ln", None), ("s3ev", em.cnt["s3ev"])]
                        if tc == 0:
                            wl.append(("s3ld", base))
                        em.op("tensor", l2tr, waits=wl, inc="s3tr")

                        def l2ev(e, tc=tc):
                            l8 = ln2T.ap().rearrange("p (c n) -> p c n", c=8)
                            e.tensor_copy(l8[:, 0:4, tc * 128:(tc + 1) * 128],
                                          ps3_lnT[:, 0:512].rearrange("p (c n) -> p c n", c=4))
                            return e.tensor_copy(
                                l8[:, 4:8, tc * 128:(tc + 1) * 128],
                                ps3_lnT[:, 512:1024].rearrange("p (c n) -> p c n", c=4))
                        em.op("vector", l2ev, waits=[("s3tr", None)], inc="s3ev")

                        def lgmm(e, tc=tc):
                            for ec in range(8):
                                inst = e.matmul(
                                    ps3_lg[:],
                                    ln2T[:, ec * 2048 + tc * 128:ec * 2048 + (tc + 1) * 128],
                                    gw_sb[:, ec * 8:(ec + 1) * 8],
                                    start=(ec == 0), stop=(ec == 7))
                            return inst
                        em.op("tensor", lgmm,
                              waits=[("s3ev", None), ("s3lgev", em.cnt["s3lgev"])],
                              inc="s3lg")
                        em.op("scalar", lambda e, xb=xb: e.copy(
                            lgsb[:, xb * 8:(xb + 1) * 8], ps3_lg[:]),
                            waits=[("s3lg", None)], inc="s3lgev")
                        em.op("vector", lambda e, xb=xb, so=so: e.reduce_max(
                            st3[:, so + 12:so + 13], lgsb[:, xb * 8:(xb + 1) * 8],
                            axis=AX, negate=True),
                            waits=[("s3lgev", None)], inc="s3sm")
                        em.op("scalar", lambda e, xb=xb, so=so: e.activation(
                            msb[:, xb * 8:(xb + 1) * 8], lgsb[:, xb * 8:(xb + 1) * 8],
                            AF.Exp, bias=st3[:, so + 12:so + 13]),
                            waits=[("s3sm", None), ("s3mt", max(0, tc - 1))],
                            inc="s3ex")

                        p8c = lambda xb=xb: msb[:, xb * 8:(xb + 1) * 8]
                        em.op("vector", lambda e, xb=xb, so=so: e.reduce_sum(
                            st3[:, so + 13:so + 14], p8c(xb), axis=AX),
                            waits=[("s3ex", None)], inc="s3tp")
                        em.op("vector", lambda e, so=so: e.reciprocal(
                            st3[:, so + 14:so + 15], st3[:, so + 13:so + 14]),
                            waits=[("s3tp", None)], inc="s3tp")
                        em.op("vector", lambda e, xb=xb, so=so: e.tensor_scalar_mul(
                            p8c(xb), p8c(xb), st3[:, so + 14:so + 15]),
                            waits=[("s3tp", None)], inc="s3tp")
                        em.op("vector", lambda e, xb=xb, so=so: e.reduce_max(
                            st3[:, so + 13:so + 14], p8c(xb), axis=AX),
                            waits=[("s3tp", None)], inc="s3tp")
                        em.op("vector", lambda e, xb=xb, so=so: e.tensor_single_scalar(
                            mwork[:], p8c(xb), st3[:, so + 13:so + 14], ALU.is_ge),
                            waits=[("s3tp", None)], inc="s3tp")
                        em.op("vector", lambda e, xb=xb: e.scalar_tensor_tensor(
                            mwork[:], mwork[:], -1e30, p8c(xb), ALU.mult, ALU.add),
                            waits=[("s3tp", None)], inc="s3tp")
                        em.op("vector", lambda e, so=so: e.reduce_max(
                            st3[:, so + 13:so + 14], mwork[:], axis=AX),
                            waits=[("s3tp", None)], inc="s3tp")
                        em.op("vector", lambda e, xb=xb, so=so: e.tensor_single_scalar(
                            mwork[:], p8c(xb), st3[:, so + 13:so + 14], ALU.is_ge),
                            waits=[("s3tp", None)], inc="s3tp")
                        em.op("vector", lambda e, xb=xb: e.tensor_tensor(
                            p8c(xb), p8c(xb), mwork[:], ALU.mult),
                            waits=[("s3tp", None)], inc="s3m")
                        def mtr(e, xb=xb):
                            for ex in range(8):
                                inst = e.matmul(
                                    ps3_mt[0:1, ex * 128:(ex + 1) * 128],
                                    msb[:, xb * 8 + ex:xb * 8 + ex + 1], idf3[:],
                                    start=(ex == 0 or ex == 4), stop=True)
                            return inst
                        em.op("tensor", mtr,
                              waits=[("s3m", None), ("s3mtev", em.cnt["s3mtev"])],
                              inc="s3mt")
                        em.op("vector", lambda e, tc=tc: e.tensor_copy(
                            mT_sb.ap().rearrange("p (a n) -> p a n", a=8)[:, :, tc * 128:(tc + 1) * 128],
                            ps3_mt.ap().rearrange("p (a n) -> p a n", a=8)),
                            waits=[("s3mt", None)], inc="s3mtev")
                    em.wait("gpsimd", "s3xw", 16 * 16)
                    em.wait("vector", "s3mtev", 16)

                for which in ENGS:
                    getattr(blk3, which)(
                        (lambda w: lambda h: phase3(Em(w, h, sems3)))(which))

        nc.all_engine_barrier()

        if phases < 4:
            _emit_dumps(nc, dump, dict(KS=KS, QS=QS, VS=VS, GS=GS, PP=PP, PR=PR, X2S=X2S, LNS=LNS, VS2=VS2, HS2=HS2, OS2=OS2))
            return nc
        with contextlib.ExitStack() as _st:
            ones_f = _st.enter_context(nc.sbuf_tensor("ones_f", [1, 128], BF16))
            eb_sb = _st.enter_context(nc.sbuf_tensor("eb_sb", [128, 64], F32))
            ew_sb = _st.enter_context(nc.sbuf_tensor("ew_sb", [128, 2 * 8192], BF16))
            silu_sb = _st.enter_context(nc.sbuf_tensor("silu_sb", [128, 2 * 512], BF16))
            mbc_sb = _st.enter_context(nc.sbuf_tensor("mbc_sb", [128, 512], F32))
            prod_sb = _st.enter_context(nc.sbuf_tensor("prod_sb", [128, 512], F32))
            ps4_pre = _st.enter_context(nc.psum_tensor("ps4_pre", [128, 2 * 512], F32))
            ps4_mbc = _st.enter_context(nc.psum_tensor("ps4_mbc", [128, 512], F32))
            s4_names = ["s4ld", "s4mbc", "s4mm", "s4si", "s4ac", "s4mf", "s4pr", "s4mb2"]
            if True:
                stack = _st
                sems4 = {n: stack.enter_context(nc.semaphore(n)) for n in s4_names}
                blk4 = stack.enter_context(nc.Block())

                def phase4b(em: Em):
                    em.op("gpsimd", lambda e: e.dma_start(ones_f[:], ones_h.ap()[:]),
                          inc="s4ld", incn=16)
                    em.op("sync", lambda e: e.dma_start(eb_sb[:], ap_in["eb_r"][:]),
                          inc="s4ld", incn=16)
                    em.op("gpsimd", lambda e: e.dma_start(
                        ew_sb[:, 0:8192].rearrange("p (c n) -> p c n", c=8),
                        ap_in["ewT"][0].rearrange("(c p) n -> p c n", p=128)),
                        inc="s4ld", incn=16)
                    for ex in range(NEX):
                        if ex + 1 < NEX:
                            em.op("gpsimd", lambda e, ex=ex: e.dma_start(
                                ew_sb[:, ((ex + 1) % 2) * 8192:(((ex + 1) % 2) + 1) * 8192].rearrange("p (c n) -> p c n", c=8),
                                ap_in["ewT"][ex + 1].rearrange("(c p) n -> p c n", p=128)),
                                waits=[("s4mm", 32 * ex)], inc="s4ld", incn=16)
                        eoff = (ex % 2) * 8192
                        for tg in range(4):
                            em.op("tensor", lambda e, ex=ex, tg=tg: e.matmul(
                                ps4_mbc[:], ones_f[:],
                                mT_sb[0:1, ex * 2048 + tg * 512:ex * 2048 + (tg + 1) * 512],
                                start=True, stop=True),
                                waits=[("s4ld", 48), ("s4mf", None)],
                                inc="s4mbc")
                            em.op("vector", lambda e: e.tensor_copy(mbc_sb[:], ps4_mbc[:]),
                                  waits=[("s4mbc", None)], inc="s4mb2")
                            for dc in range(8):
                                gi = em.cnt["s4mm"]

                                def premm(e, eoff=eoff, tg=tg, dc=dc, gi=gi):
                                    pb = (gi % 2) * 512
                                    for ec in range(8):
                                        inst = e.matmul(
                                            ps4_pre[:, pb:pb + 512],
                                            ew_sb[:, eoff + ec * 1024 + dc * 128:eoff + ec * 1024 + (dc + 1) * 128],
                                            ln2T[:, ec * 2048 + tg * 512:ec * 2048 + (tg + 1) * 512],
                                            start=(ec == 0), stop=(ec == 7))
                                    return inst
                                em.op("tensor", premm,
                                      waits=[("s4si", max(0, gi - 1)),
                                             ("s4ld", 48 + 16 * ex)],
                                      inc="s4mm")
                                wsi = [("s4mm", None), ("s4ac", max(0, gi - 1))]
                                if gi == 0:
                                    wsi.append(("s4ld", 48))
                                em.op("scalar", lambda e, ex=ex, dc=dc, gi=gi: e.activation(
                                    silu_sb[:, (gi % 2) * 512:((gi % 2) + 1) * 512],
                                    ps4_pre[:, (gi % 2) * 512:((gi % 2) + 1) * 512],
                                    AF.Silu, bias=eb_sb[:, ex * 8 + dc:ex * 8 + dc + 1]),
                                    waits=wsi, inc="s4si")

                                if ex == 0:
                                    em.op("vector", lambda e, tg=tg, dc=dc, gi=gi: e.tensor_tensor(
                                        acc[:, dc * 2048 + tg * 512:dc * 2048 + (tg + 1) * 512],
                                        silu_sb[:, (gi % 2) * 512:((gi % 2) + 1) * 512],
                                        mbc_sb[:], ALU.mult),
                                        waits=[("s4si", None), ("s4mb2", None)], inc="s4ac")
                                else:
                                    em.op("vector", lambda e, gi=gi: e.tensor_tensor(
                                        prod_sb[:], silu_sb[:, (gi % 2) * 512:((gi % 2) + 1) * 512],
                                        mbc_sb[:], ALU.mult),
                                        waits=[("s4si", None), ("s4mb2", None)], inc="s4pr")
                                    em.op("vector", lambda e, tg=tg, dc=dc: e.tensor_tensor(
                                        acc[:, dc * 2048 + tg * 512:dc * 2048 + (tg + 1) * 512],
                                        acc[:, dc * 2048 + tg * 512:dc * 2048 + (tg + 1) * 512],
                                        prod_sb[:], ALU.add),
                                        waits=[("s4pr", None)], inc="s4ac")
                            em.op("vector", lambda e: e.nop(), inc="s4mf")
                    em.wait("vector", "s4ac", 256)

                for which in ENGS:
                    getattr(blk4, which)(
                        (lambda w: lambda h: phase4b(Em(w, h, sems4)))(which))

        _ln2t_stack.close()
        nc.all_engine_barrier()
        if phases < 5:
            _emit_dumps(nc, dump, dict(KS=KS, QS=QS, VS=VS, GS=GS, PP=PP, PR=PR, X2S=X2S, LNS=LNS, VS2=VS2, HS2=HS2, OS2=OS2))
            return nc

        with contextlib.ExitStack() as _st:
            accb = _st.enter_context(nc.sbuf_tensor("accb", [128, 16384], BF16))
            cw_sb = _st.enter_context(nc.sbuf_tensor("cw_sb", [128, 8192], BF16))
            x2r = _st.enter_context(nc.sbuf_tensor("x2r", [128, 2 * 1024], F32))
            yst = _st.enter_context(nc.sbuf_tensor("yst", [128, 2 * 1024], F32))
            ps5 = _st.enter_context(nc.psum_tensor("ps5", [128, 2 * 512], F32))
            s5_names = ["s5ld", "s5cast", "s5mm", "s5ev", "s5out"]
            if True:
                stack = _st
                sems5 = {n: stack.enter_context(nc.semaphore(n)) for n in s5_names}
                blk5 = stack.enter_context(nc.Block())

                def phase4c(em: Em):
                    em.op("gpsimd", lambda e: e.dma_start(
                        cw_sb.ap().rearrange("p (c n) -> p c n", c=8),
                        ap_in["cwT"].rearrange("(c p) n -> p c n", p=128)),
                        inc="s5ld", incn=16)
                    x2v = X2S.rearrange("(c p) e -> p c e", p=128)
                    for t0 in range(2):
                        em.op("sync", lambda e, t0=t0: e.dma_start(
                            x2r[:, t0 * 1024:(t0 + 1) * 1024], x2v[:, t0, :]),
                            inc="s5ld", incn=16)
                    for i in range(16):
                        em.op("vector", lambda e, i=i: e.tensor_copy(
                            accb[:, i * 1024:(i + 1) * 1024],
                            acc[:, i * 1024:(i + 1) * 1024]),
                            inc="s5cast")
                    for tc in range(16):
                        xb = tc % 2
                        if tc + 2 < 16:
                            em.op("sync", lambda e, tc=tc, xb=xb: e.dma_start(
                                x2r[:, xb * 1024:(xb + 1) * 1024], x2v[:, tc + 2, :]),
                                waits=[("s5ev", 2 * (tc + 1))], inc="s5ld", incn=16)
                        for half in range(2):
                            p = em.cnt["s5mm"]

                            def cbmm(e, tc=tc, half=half, p=p):
                                pb = (p % 2) * 512
                                for dmc in range(8):
                                    inst = e.matmul(
                                        ps5[:, pb:pb + 512],
                                        accb[:, dmc * 2048 + tc * 128:dmc * 2048 + (tc + 1) * 128],
                                        cw_sb[:, dmc * 1024 + half * 512:dmc * 1024 + (half + 1) * 512],
                                        start=(dmc == 0), stop=(dmc == 7))
                                return inst
                            em.op("tensor", cbmm,
                                  waits=[("s5cast", 16), ("s5ld", 48),
                                         ("s5ev", max(0, p - 1))],
                                  inc="s5mm")
                            em.op("vector", lambda e, half=half, p=p, xb=xb, tc=tc: e.tensor_tensor(
                                yst[:, xb * 1024 + half * 512:xb * 1024 + (half + 1) * 512],
                                ps5[:, (p % 2) * 512:((p % 2) + 1) * 512],
                                x2r[:, xb * 1024 + half * 512:xb * 1024 + (half + 1) * 512],
                                ALU.add),
                                waits=[("s5mm", None), ("s5ld", 32 + 16 * tc),
                                       ("s5out", 16 * max(0, tc - 1))],
                                inc="s5ev")
                        em.op("gpsimd", lambda e, tc=tc, xb=xb: e.dma_start(
                            y.rearrange("(c p) e -> p c e", p=128)[:, tc, :],
                            yst[:, xb * 1024:(xb + 1) * 1024]),
                            waits=[("s5ev", None)], inc="s5out", incn=16)
                    em.wait("gpsimd", "s5out", 16 * 16)

                for which in ENGS:
                    getattr(blk5, which)(
                        (lambda w: lambda h: phase4c(Em(w, h, sems5)))(which))

    nc.all_engine_barrier()
    _emit_dumps(nc, dump, dict(KS=KS, QS=QS, VS=VS, GS=GS, PP=PP, PR=PR, X2S=X2S, LNS=LNS, VS2=VS2, HS2=HS2, OS2=OS2))
    return nc


def _emit_dumps(nc, dump, scratch):
    if dump:
        nc.all_engine_barrier()
    for nm in dump:
        t = scratch[nm]
        shp = list(t.tensor.shape)
        dt = t.tensor.dtype
        out = nc.dram_tensor("dbg_" + nm, shp, dt, kind="ExternalOutput").ap()
        nchk = shp[0] // 128
        with (
            nc.sbuf_tensor("dmp_" + nm, [128, shp[1]], dt) as dmp,
            nc.semaphore("sd_" + nm) as sd,
            nc.Block() as blkd,
        ):
            def cpy(e, t=t, out=out, dmp=dmp, shp=shp, sd=sd, nchk=nchk):
                tv = t.rearrange("(c p) n -> p c n", p=128)
                ov = out.rearrange("(c p) n -> p c n", p=128)
                for c in range(nchk):
                    e.dma_start(dmp[:], tv[:, c, :]).then_inc(sd, 16)
                    e.wait_ge(sd, 32 * c + 16)
                    e.dma_start(ov[:, c, :], dmp[:]).then_inc(sd, 16)
                    e.wait_ge(sd, 32 * c + 32)
            blkd.sync(cpy)


_NC_CACHE = {}


def kernel(L0_attn_hidden_states, L0_attn_w_qkv, L0_attn_b_qkv, L0_attn_w_g,
           L0_attn_b_g, L0_attn_w_p, L0_attn_b_p, L0_ffn_gate_w,
           L0_ffn_expert_w, L0_ffn_expert_b, L0_ffn_combine_w, _dump=()):
    x = np.asarray(L0_attn_hidden_states, dtype=np.float32)
    args = [np.asarray(a, dtype=np.float32) for a in
            (L0_attn_w_qkv, L0_attn_b_qkv, L0_attn_w_g, L0_attn_b_g,
             L0_attn_w_p, L0_attn_b_p, L0_ffn_gate_w, L0_ffn_expert_w,
             L0_ffn_expert_b, L0_ffn_combine_w)]
    in_maps = [_stage_core(c, x, *args) for c in range(NCORE)]
    key = tuple(sorted(_dump))
    if key not in _NC_CACHE:
        _NC_CACHE[key] = build_nc(dump=_dump)
    res = run_bass_kernel_spmd(_NC_CACHE[key], in_maps, core_ids=list(range(NCORE)))
    out = np.empty((B, T, E), dtype=np.float32)
    for c in range(NCORE):
        b, q = c // 4, c % 4
        out[b, q * TOWN:(q + 1) * TOWN] = res.results[c]["y"]
    if _dump:
        return out, [{k: np.asarray(v) for k, v in r.items() if k.startswith("dbg_")}
                     for r in res.results]
    return out


# version 06
# speedup vs baseline: 1.0605x; 1.0605x over previous
import contextlib
import math
import numpy as np

import concourse.bass as bass
import concourse.mybir as mybir
from concourse.bass_utils import run_bass_kernel_spmd

F32 = mybir.dt.float32
BF16 = mybir.dt.bfloat16
AF = mybir.ActivationFunctionType
ALU = mybir.AluOpType
AX = mybir.AxisListType.X

B, T, E = 2, 8192, 1024
H, DK, DV = 8, 128, 256
NCORE = 8
NCHUNK = T // 128
NGRP = T // 512
NST = 16
TOWN = 2048
EPS = 1e-5
NEX = 8

ENGS = ("sync", "gpsimd", "scalar", "vector", "tensor")


def _gammas():
    lo, hi = math.log(1.0 / 32.0), math.log(1.0 / 512.0)
    return 1.0 - np.exp(np.linspace(lo, hi, H))


def _perm_cols(h):
    return [h * DK + d for d in range(0, DK, 2)] + [h * DK + d for d in range(1, DK, 2)]


def _tables_for(heads):
    D = _gammas()
    t = np.arange(T, dtype=np.float64)
    out = [np.empty((T, 128), np.float64) for _ in range(4)]
    for hl, h in enumerate(heads):
        m = np.arange(64, dtype=np.float64)
        j = h * 64 + m
        invf = 10000.0 ** (-j / 512.0)
        sv = (2.0 * j + 0.4 * E) / (1.4 * E)
        ang = t[:, None] * invf[None, :]
        c, s = np.cos(ang), np.sin(ang)
        scq = sv[None, :] ** (t[:, None] / 512.0)
        sck = sv[None, :] ** (-t[:, None] / 512.0)
        Dh = float(D[h])
        a = Dh ** (t % 512) / np.sqrt((1.0 - Dh ** (t + 1)) / (1.0 - Dh))
        bk = Dh ** (-(t % 128)) / math.sqrt(DK)
        sl = slice(hl * 64, hl * 64 + 64)
        out[0][:, sl] = c * scq * a[:, None]
        out[1][:, sl] = s * scq * a[:, None]
        out[2][:, sl] = c * sck * bk[:, None]
        out[3][:, sl] = s * sck * bk[:, None]
    return [x.astype(np.float32) for x in out]


def _s_tab_for(heads):
    D = _gammas()
    vals = np.zeros(1088, np.float64)
    for hl, h in enumerate(heads):
        for st in range(NST):
            base = hl * 544 + 2 * st * (st + 1)
            for jb in range(4 * (st + 1)):
                vals[base + jb] = math.exp(math.log(float(D[h])) * (st * 512 - jb * 128))
    v32 = vals.astype(np.float32)
    return np.ascontiguousarray(np.broadcast_to(v32[None, :], (128, 1088)))


def _stage_core(c, x, wqkv, bqkv, wg, bg, wp, bp, gw, ew, eb, cw):
    b, q = c // 4, c % 4
    heads = (2 * q, 2 * q + 1)
    h0, h1 = heads
    qcols = _perm_cols(h0) + _perm_cols(h1)
    kcols = [E + d for d in qcols]
    vcols = list(range(2 * E + h0 * DV, 2 * E + (h0 + 1) * DV)) + \
            list(range(2 * E + h1 * DV, 2 * E + (h1 + 1) * DV))
    gcols = list(range(h0 * DV, (h0 + 1) * DV)) + list(range(h1 * DV, (h1 + 1) * DV))
    wqkvg = np.concatenate(
        [wqkv[:, qcols], wqkv[:, kcols], wqkv[:, vcols], wg[:, gcols]], axis=1)
    bvec = np.concatenate([bqkv[qcols], bqkv[kcols], bqkv[vcols], bg[gcols]])
    qC, qS, kC, kS = _tables_for(heads)
    wp_rows = [d - 2 * E for d in vcols]
    return {
        "x_full": np.ascontiguousarray(x[b]),
        "x_own": np.ascontiguousarray(x[b, q * TOWN:(q + 1) * TOWN]),
        "wqkvg": np.ascontiguousarray(wqkvg),
        "bias_bc": np.ascontiguousarray(np.broadcast_to(bvec[None, :], (128, 1536))),
        "qtabC": qC, "qtabS": qS, "ktabC": kC, "ktabS": kS,
        "s_tab": _s_tab_for(heads),
        "wp_slice": np.ascontiguousarray(wp[wp_rows, :]),
        "bp_bc": np.ascontiguousarray(np.broadcast_to(bp[None, :], (128, E))),
        "gwT": np.ascontiguousarray(gw.T),
        "ewT": np.ascontiguousarray(ew.transpose(0, 2, 1)),
        "eb_r": np.ascontiguousarray(
            eb.reshape(NEX, 8, 128).transpose(2, 0, 1).reshape(128, 64)),
        "cwT": np.ascontiguousarray(cw.T),
    }


class Em:

    def __init__(self, which, handle, sems):
        self.which = which
        self.h = handle
        self.sems = sems
        self.cnt = {k: 0 for k in sems}

    def op(self, eng, fn, waits=(), inc=None, incn=1):
        if self.which == eng:
            for s, v in waits:
                val = self.cnt[s] if v is None else v
                if val > 0:
                    self.h.wait_ge(self.sems[s], val)
            inst = fn(self.h)
            if inc is not None:
                assert inst is not None, f"op for {inc} returned no instruction"
                inst.then_inc(self.sems[inc], incn)
        if inc is not None:
            self.cnt[inc] += incn

    def wait(self, eng, sem, v=None):
        if self.which == eng:
            val = self.cnt[sem] if v is None else v
            if val > 0:
                self.h.wait_ge(self.sems[sem], val)


def _ln_stats_ops(st, so, xc, sq):
    def v_sq(e):
        return e.tensor_tensor(sq[:], xc, xc, ALU.mult)

    def v_red(e):
        e.reduce_sum(st[:, so:so + 1], xc, axis=AX, negate=True)
        return e.reduce_sum(st[:, so + 1:so + 2], sq[:], axis=AX)

    def s_stats(e):
        e.mul(st[:, so + 4:so + 5], st[:, so:so + 1], 1.0 / 1024.0)
        e.activation(st[:, so + 5:so + 6], st[:, so + 1:so + 2],
                     AF.Identity, scale=1.0 / 1024.0, bias=np.float32(EPS))
        e.activation(st[:, so + 6:so + 7], st[:, so + 4:so + 5], AF.Square)
        e.copy(st[:, so + 9:so + 10], st[:, so + 5:so + 6])
        return e.activation(st[:, so + 7:so + 8], st[:, so + 6:so + 7],
                            AF.Identity, scale=-1.0, bias=st[:, so + 5:so + 6])

    def v_recip(e):
        return e.reciprocal(st[:, so + 2:so + 3], st[:, so + 7:so + 8])
    return v_sq, v_red, s_stats, v_recip


def _lncol(sub, ec):
    return sub * 1024 + (ec * 128 if ec < 4 else 512 + (ec - 4) * 128)


def build_nc(dump=(), phases=5):
    nc = bass.Bass("TRN2", target_bir_lowering=False, num_devices=NCORE)

    ap_in = {}
    for name, shape in [
        ("x_full", [T, E]), ("x_own", [TOWN, E]), ("wqkvg", [E, 1536]),
        ("bias_bc", [128, 1536]), ("qtabC", [T, 128]), ("qtabS", [T, 128]),
        ("ktabC", [T, 128]), ("ktabS", [T, 128]), ("s_tab", [128, 1088]),
        ("wp_slice", [512, E]), ("bp_bc", [128, E]), ("gwT", [E, 8]),
        ("ewT", [NEX, E, E]), ("eb_r", [128, 64]), ("cwT", [E, E]),
    ]:
        ap_in[name] = nc.dram_tensor(name, shape, F32, kind="ExternalInput").ap()
    y = nc.dram_tensor("y", [TOWN, E], F32, kind="ExternalOutput").ap()

    KS = nc.dram_tensor("KS", [256, T], BF16, kind="Internal").ap()
    QS = nc.dram_tensor("QS", [256, T], BF16, kind="Internal").ap()
    VS = nc.dram_tensor("VS", [T, 512], BF16, kind="Internal").ap()
    GS = nc.dram_tensor("GS", [T, 512], BF16, kind="Internal").ap()
    PP = nc.dram_tensor("PP", [T, E], F32, kind="Internal").ap()
    PR = nc.dram_tensor("PR", [TOWN, E], F32, kind="Internal").ap()
    X2S = nc.dram_tensor("X2S", [TOWN, E], F32, kind="Internal").ap()
    LNS = nc.dram_tensor("LNS", [T, E], BF16, kind="Internal").ap()
    VS2 = nc.dram_tensor("VS2", [T, 512], BF16, kind="Internal").ap()
    HS2 = nc.dram_tensor("HS2", [512, T], BF16, kind="Internal").ap()
    OS2 = nc.dram_tensor("OS2", [T, 512], F32, kind="Internal").ap()

    idb = nc.inline_tensor(np.eye(128, dtype=np.float32), name="idb")
    ones_h = nc.inline_tensor(np.ones((1, 128), np.float32), name="ones_h")
    tri_h = []
    jl = np.arange(128)[:, None]
    il = np.arange(512)[None, :]
    for o in range(4):
        tri_h.append(nc.inline_tensor(
            (il >= jl + o * 128).astype(np.float32), name=f"tri{o}"))

    def chunks(apx, p=128):
        return apx.rearrange("(c p) n -> p c n", p=p)

    with contextlib.ExitStack() as _st:
        wsb = _st.enter_context(nc.sbuf_tensor("wsb", [128, 8 * 1536], BF16))
        bias_sb = _st.enter_context(nc.sbuf_tensor("bias_sb", [128, 1536], F32))
        idb_sb = _st.enter_context(nc.sbuf_tensor("idb_sb", [128, 128], BF16))
        x_in = _st.enter_context(nc.sbuf_tensor("x_in", [128, 2 * 1024], F32))
        xsq = _st.enter_context(nc.sbuf_tensor("xsq", [128, 1024], F32))
        t1sb = _st.enter_context(nc.sbuf_tensor("t1sb", [128, 2 * 1024], F32))
        ln_sb = _st.enter_context(nc.sbuf_tensor("ln_sb", [128, 2 * 1024], BF16))
        lnT_sb = _st.enter_context(nc.sbuf_tensor("lnT_sb", [128, 4096], BF16))
        stats = _st.enter_context(nc.sbuf_tensor("stats", [128, 2 * 16], F32))
        tabs = _st.enter_context(nc.sbuf_tensor("tabs", [128, 2 * 4 * 512], F32))
        qkw = _st.enter_context(nc.sbuf_tensor("qkw", [128, 2 * 512], F32))
        xw = _st.enter_context(nc.sbuf_tensor("xw", [128, 1024], F32))
        qkt = _st.enter_context(nc.sbuf_tensor("qkt", [128, 2 * 512], BF16))
        gwk = _st.enter_context(nc.sbuf_tensor("gwk", [128, 2 * 512], F32))
        vstage = _st.enter_context(nc.sbuf_tensor("vstage", [128, 2048], BF16))
        gstage = _st.enter_context(nc.sbuf_tensor("gstage", [128, 2048], BF16))
        qstage = _st.enter_context(nc.sbuf_tensor("qstage", [128, 1024], BF16))
        kstage = _st.enter_context(nc.sbuf_tensor("kstage", [128, 1024], BF16))
        ps_lnT = _st.enter_context(nc.psum_tensor("ps_lnT", [128, 1024], F32))
        ps_qkv = _st.enter_context(nc.psum_tensor("ps_qkv", [128, 2 * 512], F32))
        ps_tr = _st.enter_context(nc.psum_tensor("ps_tr", [128, 2048], F32))
        s_names = ["sx", "stab", "swld", "sa", "sb", "sln", "slntr", "slnev",
                   "sqkv", "sqkev", "ssilu", "sxpos", "sqktr", "strev",
                   "swout", "sxfree", "sdbg", "sc", "sxp1"]
        if True:
            stack = _st
            sems = {n: stack.enter_context(nc.semaphore(n)) for n in s_names}
            blk = stack.enter_context(nc.Block())

            def phase1(em: Em):
                xv = chunks(ap_in["x_full"])
                tabins = [ap_in["qtabC"], ap_in["qtabS"], ap_in["ktabC"], ap_in["ktabS"]]
                tb_all = tabs.ap().rearrange("p (b k s n) -> p b k s n", b=2, k=4, s=4)

                em.op("gpsimd", lambda e: e.dma_start(
                    wsb.ap().rearrange("p (c n) -> p c n", c=8),
                    chunks(ap_in["wqkvg"])), inc="swld", incn=16)
                em.op("sync", lambda e: e.dma_start(bias_sb[:], ap_in["bias_bc"][:]),
                      inc="swld", incn=16)
                em.op("gpsimd", lambda e: e.dma_start(idb_sb[:], idb.ap()[:]),
                      inc="swld", incn=16)
                for t0 in range(2):
                    em.op("sync", lambda e, t0=t0: e.dma_start(
                        x_in[:, t0 * 1024:(t0 + 1) * 1024], xv[:, t0, :]),
                        inc="sx", incn=16)
                for ti in range(4):
                    em.op("sync", lambda e, ti=ti: e.dma_start(
                        tb_all[:, 0, ti], chunks(tabins[ti])[:, 0:4, :]),
                        inc="stab", incn=16)

                for g in range(NGRP):
                    gb = g % 2
                    if g + 1 < NGRP:
                        for ti in range(4):
                            em.op("sync", lambda e, ti=ti, g=g: e.dma_start(
                                tb_all[:, (g + 1) % 2, ti],
                                chunks(tabins[ti])[:, 4 * (g + 1):4 * (g + 2), :]),
                                waits=[("sxpos", max(0, 4 * g))],
                                inc="stab", incn=16)
                    for sub in range(4):
                        tc = 4 * g + sub
                        xb = tc % 2
                        xoff = xb * 1024
                        so = xb * 16
                        if tc + 2 < NCHUNK:
                            em.op("sync", lambda e, tc=tc, xb=xb: e.dma_start(
                                x_in[:, xb * 1024:(xb + 1) * 1024], xv[:, tc + 2, :]),
                                waits=[("sxfree", tc + 1)], inc="sx", incn=16)
                        v_sq, v_red, s_stats, v_recip = _ln_stats_ops(stats, so, x_in[:, xoff:xoff + 1024], xsq)
                        em.op("vector", v_sq,
                              waits=[("sx", 16 * (tc + 1)), ("sln", max(0, tc - 1))],
                              inc="sa")
                        em.op("vector", v_red, waits=[("sa", None)], inc="sb")
                        em.op("scalar", s_stats, waits=[("sb", None)], inc="sc")
                        em.op("scalar", lambda e, xoff=xoff, so=so: e.activation(
                            t1sb[:, xoff:xoff + 1024], x_in[:, xoff:xoff + 1024],
                            AF.Identity, bias=stats[:, so + 4:so + 5]),
                            inc="sxfree")
                        em.op("vector", v_recip, waits=[("sc", None)], inc="sb")
                        def s_ln(e, xoff=xoff, so=so):
                            e.sqrt(stats[:, so + 8:so + 9], stats[:, so + 2:so + 3])
                            e.copy(stats[:, so + 9:so + 10], stats[:, so + 8:so + 9])
                            return e.activation(
                                ln_sb[:, xoff:xoff + 1024], t1sb[:, xoff:xoff + 1024],
                                AF.Identity, scale=stats[:, so + 8:so + 9])
                        em.op("scalar", s_ln,
                              waits=[("sb", None), ("slntr", max(0, tc - 1))], inc="sln")

                        def lntr(e, xoff=xoff):
                            for ec in range(8):
                                col = (0 if ec < 4 else 512) + (ec % 4) * 128
                                inst = e.matmul(
                                    ps_lnT[:, col:col + 128],
                                    ln_sb[:, xoff + ec * 128:xoff + (ec + 1) * 128],
                                    idb_sb[:], start=(ec % 4 == 0), stop=True)
                            return inst
                        wl = [("sln", None), ("slnev", em.cnt["slnev"])]
                        if tc == 0:
                            wl.append(("swld", 48))
                        em.op("tensor", lntr, waits=wl, inc="slntr")

                        def lnev(e, sub=sub):
                            e.tensor_copy(lnT_sb[:, sub * 1024:sub * 1024 + 512],
                                          ps_lnT[:, 0:512])
                            return e.tensor_copy(
                                lnT_sb[:, sub * 1024 + 512:sub * 1024 + 1024],
                                ps_lnT[:, 512:1024])
                        em.op("vector", lnev,
                              waits=[("slntr", None), ("sqkv", 12 * max(0, g))],
                              inc="slnev")

                    for sub in range(4):
                        tc = 4 * g + sub
                        for grp in range(3):
                            gi = em.cnt["sqkv"]

                            def qkvmm(e, sub=sub, grp=grp, gi=gi):
                                pb = (gi % 2) * 512
                                for ec in range(8):
                                    inst = e.matmul(
                                        ps_qkv[:, pb:pb + 512],
                                        lnT_sb[:, _lncol(sub, ec):_lncol(sub, ec) + 128],
                                        wsb[:, ec * 1536 + grp * 512:ec * 1536 + (grp + 1) * 512],
                                        start=(ec == 0), stop=(ec == 7))
                                return inst
                            em.op("tensor", qkvmm,
                                  waits=[("slnev", 4 * g + sub + 1),
                                         ("sqkev", max(0, gi - 1))],
                                  inc="sqkv")

                            def qkev(e, sub=sub, grp=grp, gi=gi):
                                pb = (gi % 2) * 512
                                bb = bias_sb[:, grp * 512:(grp + 1) * 512]
                                if grp == 0:
                                    out = qkw[:, (sub % 2) * 512:((sub % 2) + 1) * 512]
                                elif grp == 1:
                                    out = vstage[:, sub * 512:(sub + 1) * 512]
                                else:
                                    out = gwk[:, (sub % 2) * 512:((sub % 2) + 1) * 512]
                                return e.tensor_tensor(out, ps_qkv[:, pb:pb + 512], bb, ALU.add)
                            wv = [("sqkv", None)]
                            if g == 0 and sub == 0 and grp == 0:
                                wv.append(("swld", 48))
                            if grp == 1 and sub == 0:
                                wv.append(("swout", 64 * g))
                            if grp == 2:
                                wv.append(("ssilu", max(0, tc - 1)))
                            em.op("vector", qkev, waits=wv, inc="sqkev")
                        em.op("scalar", lambda e, sub=sub: e.activation(
                            gstage[:, sub * 512:(sub + 1) * 512],
                            gwk[:, (sub % 2) * 512:((sub % 2) + 1) * 512], AF.Silu),
                            waits=[("sqkev", None)], inc="ssilu")

                        def xpos_prod(e, sub=sub, gb=gb):
                            qk4 = qkw.ap().rearrange("p (w b h x) -> p w b h x", w=2, b=4, h=2)
                            xw8 = xw.ap().rearrange("p (b h x) -> p b h x", b=8, h=2)
                            wv_ = sub % 2
                            for part in range(2):
                                C = tb_all[:, gb, 2 * part, sub].rearrange("p (h x) -> p h x", h=2)
                                S = tb_all[:, gb, 2 * part + 1, sub].rearrange("p (h x) -> p h x", h=2)
                                qE = qk4[:, wv_, 2 * part:2 * part + 2, 0]
                                qO = qk4[:, wv_, 2 * part:2 * part + 2, 1]
                                o4 = 4 * part
                                e.tensor_tensor(xw8[:, o4 + 0:o4 + 2, 0], qE, C, ALU.mult)
                                e.tensor_tensor(xw8[:, o4 + 0:o4 + 2, 1], qO, C, ALU.mult)
                                e.tensor_tensor(xw8[:, o4 + 2:o4 + 4, 0], qO, S, ALU.mult)
                                inst = e.tensor_tensor(xw8[:, o4 + 2:o4 + 4, 1], qE, S, ALU.mult)
                            return inst
                        em.op("vector", xpos_prod,
                              waits=[("stab", 16 * 4 * (g + 1)), ("sqktr", max(0, tc - 1)),
                                     ("sqkev", None)],
                              inc="sxp1")
                        def xpos_comb(e, sub=sub):
                            ot4 = qkt.ap().rearrange("p (w b h x) -> p w b h x", w=2, b=4, h=2)
                            xw8 = xw.ap().rearrange("p (b h x) -> p b h x", b=8, h=2)
                            wv_ = sub % 2
                            for part in range(2):
                                o4 = 4 * part
                                e.tensor_tensor(ot4[:, wv_, 2 * part:2 * part + 2, 0],
                                                xw8[:, o4 + 0:o4 + 2, 0], xw8[:, o4 + 2:o4 + 4, 0], ALU.subtract)
                                inst = e.tensor_tensor(ot4[:, wv_, 2 * part:2 * part + 2, 1],
                                                       xw8[:, o4 + 0:o4 + 2, 1], xw8[:, o4 + 2:o4 + 4, 1], ALU.add)
                            return inst
                        em.op("vector", xpos_comb, waits=[("sxp1", None)], inc="sxpos")

                        def qktr(e, sub=sub):
                            wv_ = sub % 2
                            for cc in range(4):
                                inst = e.matmul(
                                    ps_tr[:, cc * 512 + sub * 128:cc * 512 + (sub + 1) * 128],
                                    qkt[:, wv_ * 512 + cc * 128:wv_ * 512 + (cc + 1) * 128],
                                    idb_sb[:], start=(sub == 0), stop=True)
                            return inst
                        em.op("tensor", qktr,
                              waits=[("sxpos", None), ("strev", em.cnt["strev"])],
                              inc="sqktr")

                    def trev(e):
                        e.tensor_copy(qstage[:, 0:512], ps_tr[:, 0:512])
                        e.tensor_copy(qstage[:, 512:1024], ps_tr[:, 512:1024])
                        e.tensor_copy(kstage[:, 0:512], ps_tr[:, 1024:1536])
                        return e.tensor_copy(kstage[:, 512:1024], ps_tr[:, 1536:2048])
                    em.op("vector", trev,
                          waits=[("sqktr", None), ("swout", 64 * g)], inc="strev")
                    for dst, src in [(QS, qstage), (KS, kstage)]:
                        em.op("gpsimd", lambda e, dst=dst, src=src, g=g: e.dma_start(
                            dst.rearrange("(c p) t -> p c t", p=128)[:, :, g * 512:(g + 1) * 512],
                            src.ap().rearrange("p (c t) -> p c t", c=2)),
                            waits=[("strev", None)], inc="swout", incn=16)
                    em.op("gpsimd", lambda e, g=g: e.dma_start(
                        VS.rearrange("(c p) v -> p c v", p=128)[:, 4 * g:4 * g + 4, :],
                        vstage.ap().rearrange("p (c v) -> p c v", c=4)),
                        waits=[("sqkev", None)], inc="swout", incn=16)
                    em.op("gpsimd", lambda e, g=g: e.dma_start(
                        GS.rearrange("(c p) v -> p c v", p=128)[:, 4 * g:4 * g + 4, :],
                        gstage.ap().rearrange("p (c v) -> p c v", c=4)),
                        waits=[("ssilu", None)], inc="swout", incn=16)
                em.wait("gpsimd", "swout", 64 * NGRP)

            for which in ENGS:
                getattr(blk, which)(
                    (lambda w: lambda h: phase1(Em(w, h, sems)))(which))

    nc.all_engine_barrier()

    if phases < 2:
        _emit_dumps(nc, dump, dict(KS=KS, QS=QS, VS=VS, GS=GS, PP=PP, PR=PR, X2S=X2S, LNS=LNS, VS2=VS2, HS2=HS2, OS2=OS2))
        return nc
    with contextlib.ExitStack() as _st:
        K2 = _st.enter_context(nc.sbuf_tensor("K2", [128, 2 * T], BF16))
        Q2 = _st.enter_context(nc.sbuf_tensor("Q2", [128, 2 * T], BF16))
        V2 = _st.enter_context(nc.sbuf_tensor("V2", [128, 2 * 64 * 256], BF16))
        G2 = _st.enter_context(nc.sbuf_tensor("G2", [128, 2 * 1024], BF16))
        ssb = _st.enter_context(nc.sbuf_tensor("ssb", [128, 1088], F32))
        tri_sb = _st.enter_context(nc.sbuf_tensor("tri_sb", [128, 4 * 512], BF16))
        idb2 = _st.enter_context(nc.sbuf_tensor("idb2", [128, 128], BF16))
        wp_sb = _st.enter_context(nc.sbuf_tensor("wp_sb", [128, 4 * 1024], BF16))
        stsb = _st.enter_context(nc.sbuf_tensor("stsb", [128, 2 * 512], BF16))
        osb = _st.enter_context(nc.sbuf_tensor("osb", [128, 2 * 256], F32))
        osq = _st.enter_context(nc.sbuf_tensor("osq", [128, 256], F32))
        gt_sb = _st.enter_context(nc.sbuf_tensor("gt_sb", [128, 2 * 256], BF16))
        ht_tok = _st.enter_context(nc.sbuf_tensor("ht_tok", [128, 2 * 256], BF16))
        hT_sb = _st.enter_context(nc.sbuf_tensor("hT_sb", [128, 4 * 512], BF16))
        st2 = _st.enter_context(nc.sbuf_tensor("st2", [128, 2 * 16], F32))
        ppst = _st.enter_context(nc.sbuf_tensor("ppst", [128, 4096], F32))
        ps_st = _st.enter_context(nc.psum_tensor("ps_st", [128, 2 * 512], F32))
        ps_O = _st.enter_context(nc.psum_tensor("ps_O", [128, 1024], F32))
        ps_h = _st.enter_context(nc.psum_tensor("ps_h", [128, 1024], F32))
        ps_pp = _st.enter_context(nc.psum_tensor("ps_pp", [128, 2 * 512], F32))
        s2_names = ["p2ld", "sgld", "sstmm", "sstev", "sdiag", "somm", "sgn1",
                    "sgn2", "sosub", "sgt", "sht", "shttr", "shev", "soev",
                    "sppmm", "sppev", "sppwr", "scc", "sq1", "sgn2a", "sgn2b", "sdbg2", "sgn1f"]
        if True:
            stack = _st
            sems2 = {n: stack.enter_context(nc.semaphore(n)) for n in s2_names}
            blk2 = stack.enter_context(nc.Block())

            def phase2(em: Em):
                for hl in range(2):
                    em.op("sync", lambda e, hl=hl: e.dma_start(
                        K2[:, hl * T:(hl + 1) * T],
                        KS.rearrange("(c p) t -> p c t", p=128)[:, hl, :]),
                        inc="p2ld", incn=16)
                    em.op("sync", lambda e, hl=hl: e.dma_start(
                        Q2[:, hl * T:(hl + 1) * T],
                        QS.rearrange("(c p) t -> p c t", p=128)[:, hl, :]),
                        inc="p2ld", incn=16)
                    for part in range(8):
                        em.op("sync", lambda e, hl=hl, part=part: e.dma_start(
                            V2.ap().rearrange("p (h c v) -> p h c v", h=2, c=64)[:, hl, part * 8:(part + 1) * 8],
                            VS.rearrange("(c p) v -> p c v", p=128)[:, part * 8:(part + 1) * 8, hl * 256:(hl + 1) * 256]),
                            inc="p2ld", incn=16)
                em.op("sync", lambda e: e.dma_start(ssb[:], ap_in["s_tab"][:]),
                      inc="p2ld", incn=16)
                for o in range(4):
                    em.op("gpsimd", lambda e, o=o: e.dma_start(
                        tri_sb[:, o * 512:(o + 1) * 512], tri_h[o].ap()[:]),
                        inc="p2ld", incn=16)
                em.op("gpsimd", lambda e: e.dma_start(idb2[:], idb.ap()[:]),
                      inc="p2ld", incn=16)
                em.op("gpsimd", lambda e: e.dma_start(
                    wp_sb.ap().rearrange("p (c n) -> p c n", c=4),
                    ap_in["wp_slice"].rearrange("(c p) n -> p c n", p=128)),
                    inc="p2ld", incn=16)
                ldtot = em.cnt["p2ld"]

                jdx = 0
                for st in range(NST):
                    for hl in range(2):
                        n = 2 * st + hl
                        em.op("sync", lambda e, st=st, hl=hl: e.dma_start(
                            G2[:, hl * 1024:(hl + 1) * 1024].rearrange("p (c v) -> p c v", c=4),
                            GS.rearrange("(c p) v -> p c v", p=128)[:, 4 * st:4 * st + 4, hl * 256:(hl + 1) * 256]),
                            waits=[("sgt", max(0, 4 * (n - 1)))],
                            inc="sgld", incn=16)
                        nj = 4 * (st + 1)
                        for jb in range(nj):
                            sidx = hl * 544 + 2 * st * (st + 1) + jb
                            diag = jb >= 4 * st

                            def stmm(e, st=st, hl=hl, jb=jb, jdx=jdx):
                                pb = (jdx % 2) * 512
                                return e.matmul(
                                    ps_st[:, pb:pb + 512],
                                    K2[:, hl * T + jb * 128:hl * T + (jb + 1) * 128],
                                    Q2[:, hl * T + st * 512:hl * T + (st + 1) * 512],
                                    start=True, stop=True)
                            wt = [("sstev", max(0, jdx - 1)), ("somm", max(0, jdx - 2))]
                            if jdx == 0:
                                wt.append(("p2ld", ldtot))
                            if jb == 0:
                                wt.append(("soev", n))
                            em.op("tensor", stmm, waits=wt, inc="sstmm")

                            ws = [("sstmm", None), ("somm", max(0, jdx - 1))]
                            if jdx == 0:
                                ws.append(("p2ld", ldtot))
                            em.op("scalar", lambda e, jdx=jdx, sidx=sidx: e.mul(
                                stsb[:, (jdx % 2) * 512:((jdx % 2) + 1) * 512],
                                ps_st[:, (jdx % 2) * 512:((jdx % 2) + 1) * 512],
                                ssb[:, sidx:sidx + 1]),
                                waits=ws, inc="sstev")
                            if diag:
                                o = jb - 4 * st
                                wd = [("sstev", None)]
                                if em.cnt["sdiag"] == 0:
                                    wd.append(("p2ld", ldtot))
                                em.op("vector", lambda e, jdx=jdx, o=o: e.tensor_tensor(
                                    stsb[:, (jdx % 2) * 512:((jdx % 2) + 1) * 512],
                                    stsb[:, (jdx % 2) * 512:((jdx % 2) + 1) * 512],
                                    tri_sb[:, o * 512:(o + 1) * 512], ALU.mult),
                                    waits=wd, inc="sdiag")

                            def omm(e, hl=hl, jb=jb, jdx=jdx,
                                    first=(jb == 0), last=(jb == nj - 1)):
                                pb = (jdx % 2) * 512
                                for sub in range(4):
                                    inst = e.matmul(
                                        ps_O[:, sub * 256:(sub + 1) * 256],
                                        stsb[:, pb + sub * 128:pb + (sub + 1) * 128],
                                        V2[:, hl * 16384 + jb * 256:hl * 16384 + (jb + 1) * 256],
                                        start=(first and sub % 2 == 0), stop=last)
                                return inst
                            wo = [("sstev", None)]
                            if diag:
                                wo.append(("sdiag", None))
                            em.op("tensor", omm, waits=wo, inc="somm")
                            jdx += 1
                        for sub in range(4):
                            so = (sub % 2) * 16
                            ob = (sub % 2) * 256
                            em.op("vector", lambda e, sub=sub, so=so: e.reduce_sum(
                                st2[:, so:so + 1], ps_O[:, sub * 256:(sub + 1) * 256],
                                axis=AX, negate=True),
                                waits=[("somm", None), ("sgt", max(0, em.cnt["sgt"] - 1))],
                                inc="sgn1")
                            def s_osub(e, sub=sub, so=so, ob=ob):
                                e.mul(st2[:, so + 4:so + 5], st2[:, so:so + 1], 1.0 / 256.0)
                                e.copy(st2[:, so + 9:so + 10], st2[:, so + 4:so + 5])
                                return e.activation(osb[:, ob:ob + 256], ps_O[:, sub * 256:(sub + 1) * 256],
                                                    AF.Identity, bias=st2[:, so + 4:so + 5])
                            em.op("scalar", s_osub,
                                  waits=[("sgn1", None), ("sht", max(0, em.cnt["sht"] - 1))],
                                  inc="sosub")
                            em.op("vector", lambda e, ob=ob: e.tensor_tensor(
                                osq[:], osb[:, ob:ob + 256], osb[:, ob:ob + 256], ALU.mult),
                                waits=[("sosub", None)], inc="sq1")
                            em.op("vector", lambda e, so=so: e.reduce_sum(
                                st2[:, so + 1:so + 2], osq[:], axis=AX),
                                waits=[("sq1", None)], inc="sgn2a")
                            def s_var(e, so=so):
                                e.activation(st2[:, so + 5:so + 6], st2[:, so + 1:so + 2],
                                             AF.Identity, scale=1.0 / 256.0, bias=np.float32(EPS))
                                e.activation(st2[:, so + 6:so + 7], st2[:, so + 4:so + 5], AF.Square)
                                e.copy(st2[:, so + 9:so + 10], st2[:, so + 5:so + 6])
                                return e.activation(st2[:, so + 7:so + 8], st2[:, so + 6:so + 7],
                                                    AF.Identity, scale=-1.0, bias=st2[:, so + 5:so + 6])
                            em.op("scalar", s_var, waits=[("sgn2a", None)], inc="sgn2b")
                            em.op("vector", lambda e, so=so: e.reciprocal(
                                st2[:, so + 2:so + 3], st2[:, so + 7:so + 8]),
                                waits=[("sgn2b", None)], inc="sgn2")
                            def gtile(e, sub=sub, hl=hl, so=so, ob=ob):
                                e.sqrt(st2[:, so + 8:so + 9], st2[:, so + 2:so + 3])
                                e.copy(st2[:, so + 10:so + 11], st2[:, so + 8:so + 9])
                                return e.activation(
                                    gt_sb[:, ob:ob + 256],
                                    G2[:, hl * 1024 + sub * 256:hl * 1024 + (sub + 1) * 256],
                                    AF.Identity, scale=st2[:, so + 8:so + 9])
                            em.op("scalar", gtile,
                                  waits=[("sgn2", None), ("sgld", 16 * (n + 1))],
                                  inc="sgt")
                            em.op("vector", lambda e, ob=ob: e.tensor_tensor(
                                ht_tok[:, ob:ob + 256], osb[:, ob:ob + 256],
                                gt_sb[:, ob:ob + 256], ALU.mult),
                                waits=[("sgt", None), ("shttr", max(0, em.cnt["shttr"] - 1))],
                                inc="sht")
                            def httr(e, sub=sub, ob=ob):
                                for vc in range(2):
                                    inst = e.matmul(
                                        ps_h[:, vc * 512 + sub * 128:vc * 512 + (sub + 1) * 128],
                                        ht_tok[:, ob + vc * 128:ob + (vc + 1) * 128],
                                        idb2[:], start=(sub == 0), stop=True)
                                return inst
                            wh = [("sht", None)]
                            if sub == 0:
                                wh.append(("shev", em.cnt["shev"]))
                            em.op("tensor", httr, waits=wh, inc="shttr")
                        em.op("scalar", lambda e: e.nop(), inc="soev")

                        def hev(e, hl=hl):
                            e.tensor_copy(hT_sb[:, hl * 1024:hl * 1024 + 512], ps_h[:, 0:512])
                            return e.tensor_copy(
                                hT_sb[:, hl * 1024 + 512:hl * 1024 + 1024], ps_h[:, 512:1024])
                        em.op("vector", hev,
                              waits=[("shttr", None), ("sppmm", 8 * st)],
                              inc="shev", incn=2)
                    for tc in range(4):
                        for half in range(2):
                            p = em.cnt["sppmm"]

                            def ppmm(e, tc=tc, half=half, p=p):
                                pb = (p % 2) * 512
                                for vcg in range(4):
                                    inst = e.matmul(
                                        ps_pp[:, pb:pb + 512],
                                        hT_sb[:, vcg * 512 + tc * 128:vcg * 512 + (tc + 1) * 128],
                                        wp_sb[:, vcg * 1024 + half * 512:vcg * 1024 + (half + 1) * 512],
                                        start=(vcg == 0), stop=(vcg == 3))
                                return inst
                            em.op("tensor", ppmm,
                                  waits=[("shev", None), ("sppev", max(0, p - 1))],
                                  inc="sppmm")
                            em.op("vector", lambda e, tc=tc, half=half, p=p: e.tensor_copy(
                                ppst[:, tc * 1024 + half * 512:tc * 1024 + (half + 1) * 512],
                                ps_pp[:, (p % 2) * 512:((p % 2) + 1) * 512]),
                                waits=[("sppmm", None), ("sppwr", 16 * st)],
                                inc="sppev")
                    em.op("sync", lambda e, st=st: e.dma_start(
                        PP.rearrange("(c p) e -> p c e", p=128)[:, 4 * st:4 * st + 4, :],
                        ppst.ap().rearrange("p (c e) -> p c e", c=4)),
                        waits=[("sppev", None)], inc="sppwr", incn=16)
                em.op("gpsimd", lambda e: e.collective_compute(
                    "ReduceScatter", ALU.add,
                    replica_groups=[[0, 1, 2, 3], [4, 5, 6, 7]],
                    ins=[PP], outs=[PR]),
                    waits=[("sppwr", 16 * NST)], inc="scc")
                em.wait("gpsimd", "scc", 1)

            for which in ENGS:
                getattr(blk2, which)(
                    (lambda w: lambda h: phase2(Em(w, h, sems2)))(which))

    nc.all_engine_barrier()

    if phases < 3:
        _emit_dumps(nc, dump, dict(KS=KS, QS=QS, VS=VS, GS=GS, PP=PP, PR=PR, X2S=X2S, LNS=LNS, VS2=VS2, HS2=HS2, OS2=OS2))
        return nc
    with contextlib.ExitStack() as _st:
        mT_sb = _st.enter_context(nc.sbuf_tensor("mT_sb", [1, 16384], BF16))
        acc = _st.enter_context(nc.sbuf_tensor("acc", [128, 16384], F32))
        _ln2t_stack = _st.enter_context(contextlib.ExitStack())
        ln2T = _ln2t_stack.enter_context(nc.sbuf_tensor("ln2T", [128, 16384], BF16))
        with contextlib.ExitStack() as _st:
            bp_sb = _st.enter_context(nc.sbuf_tensor("bp_sb", [128, 1024], F32))
            xo_in = _st.enter_context(nc.sbuf_tensor("xo_in", [128, 2 * 1024], F32))
            pr_in = _st.enter_context(nc.sbuf_tensor("pr_in", [128, 2 * 1024], F32))
            x2sb = _st.enter_context(nc.sbuf_tensor("x2sb", [128, 2 * 1024], F32))
            x2sq = _st.enter_context(nc.sbuf_tensor("x2sq", [128, 1024], F32))
            t2sb = _st.enter_context(nc.sbuf_tensor("t2sb", [128, 2 * 1024], F32))
            ln2c = _st.enter_context(nc.sbuf_tensor("ln2c", [128, 2 * 1024], BF16))
            st3 = _st.enter_context(nc.sbuf_tensor("st3", [128, 2 * 16], F32))
            gw_sb = _st.enter_context(nc.sbuf_tensor("gw_sb", [128, 64], BF16))
            idb3 = _st.enter_context(nc.sbuf_tensor("idb3", [128, 128], BF16))
            idf3 = _st.enter_context(nc.sbuf_tensor("idf3", [128, 128], F32))
            lgsb = _st.enter_context(nc.sbuf_tensor("lgsb", [128, 2 * 8], F32))
            msb = _st.enter_context(nc.sbuf_tensor("msb", [128, 2 * 8], F32))
            mwork = _st.enter_context(nc.sbuf_tensor("mwork", [128, 8], F32))
            ps3_lnT = _st.enter_context(nc.psum_tensor("ps3_lnT", [128, 1024], F32))
            ps3_lg = _st.enter_context(nc.psum_tensor("ps3_lg", [128, 8], F32))
            ps3_mt = _st.enter_context(nc.psum_tensor("ps3_mt", [1, 1024], F32))
            s3_names = ["s3ld", "s3x2", "s3xw", "s3a", "s3b", "s3ln", "s3tr",
                        "s3ev", "s3lg", "s3lgev", "s3sm", "s3ex", "s3m",
                        "s3mt", "s3mtev", "s3c", "s3tp"]
            if True:
                stack = _st
                sems3 = {n: stack.enter_context(nc.semaphore(n)) for n in s3_names}
                blk3 = stack.enter_context(nc.Block())

                def phase3(em: Em):
                    xov = chunks(ap_in["x_own"])
                    prv = PR.rearrange("(c p) e -> p c e", p=128)
                    em.op("sync", lambda e: e.dma_start(bp_sb[:], ap_in["bp_bc"][:]),
                          inc="s3ld", incn=16)
                    em.op("gpsimd", lambda e: e.dma_start(idb3[:], idb.ap()[:]),
                          inc="s3ld", incn=16)
                    em.op("sync", lambda e: e.dma_start(idf3[:], idb.ap()[:]),
                          inc="s3ld", incn=16)
                    em.op("gpsimd", lambda e: e.dma_start(
                        gw_sb.ap().rearrange("p (c n) -> p c n", c=8),
                        ap_in["gwT"].rearrange("(c p) n -> p c n", p=128)),
                        inc="s3ld", incn=16)
                    base = em.cnt["s3ld"]
                    for t0 in range(2):
                        em.op("sync", lambda e, t0=t0: e.dma_start(
                            xo_in[:, t0 * 1024:(t0 + 1) * 1024], xov[:, t0, :]),
                            inc="s3ld", incn=16)
                        em.op("sync", lambda e, t0=t0: e.dma_start(
                            pr_in[:, t0 * 1024:(t0 + 1) * 1024], prv[:, t0, :]),
                            inc="s3ld", incn=16)
                    for tc in range(16):
                        xb = tc % 2
                        xoff = xb * 1024
                        so = xb * 16
                        if tc + 2 < 16:
                            em.op("sync", lambda e, tc=tc, xb=xb: e.dma_start(
                                xo_in[:, xb * 1024:(xb + 1) * 1024], xov[:, tc + 2, :]),
                                waits=[("s3x2", tc + 1)], inc="s3ld", incn=16)
                            em.op("sync", lambda e, tc=tc, xb=xb: e.dma_start(
                                pr_in[:, xb * 1024:(xb + 1) * 1024], prv[:, tc + 2, :]),
                                waits=[("s3x2", tc + 1)], inc="s3ld", incn=16)

                        em.op("vector", lambda e, xoff=xoff: e.tensor_tensor(
                            x2sb[:, xoff:xoff + 1024], xo_in[:, xoff:xoff + 1024],
                            pr_in[:, xoff:xoff + 1024], ALU.add),
                            waits=[("s3ld", base + 32 * (tc + 1)),
                                   ("s3xw", 16 * max(0, tc - 1)),
                                   ("s3ln", max(0, tc - 1))],
                            inc="s3tp")
                        em.op("vector", lambda e, xoff=xoff: e.tensor_tensor(
                            x2sb[:, xoff:xoff + 1024], x2sb[:, xoff:xoff + 1024],
                            bp_sb[:], ALU.add),
                            waits=[("s3tp", None)], inc="s3x2")
                        em.op("gpsimd", lambda e, tc=tc, xoff=xoff: e.dma_start(
                            X2S.rearrange("(c p) e -> p c e", p=128)[:, tc, :],
                            x2sb[:, xoff:xoff + 1024]),
                            waits=[("s3x2", None)], inc="s3xw", incn=16)
                        v_sq, v_red, s_stats, v_recip = _ln_stats_ops(st3, so, x2sb[:, xoff:xoff + 1024], x2sq)
                        em.op("vector", v_sq, waits=[("s3x2", None)], inc="s3a")
                        em.op("vector", v_red, waits=[("s3a", None)], inc="s3b")
                        em.op("scalar", s_stats, waits=[("s3b", None)], inc="s3c")
                        em.op("scalar", lambda e, xoff=xoff, so=so: e.activation(
                            t2sb[:, xoff:xoff + 1024], x2sb[:, xoff:xoff + 1024],
                            AF.Identity, bias=st3[:, so + 4:so + 5]))
                        em.op("vector", v_recip, waits=[("s3c", None)], inc="s3b")
                        def s_ln2(e, xoff=xoff, so=so):
                            e.sqrt(st3[:, so + 8:so + 9], st3[:, so + 2:so + 3])
                            e.copy(st3[:, so + 9:so + 10], st3[:, so + 8:so + 9])
                            return e.activation(
                                ln2c[:, xoff:xoff + 1024], t2sb[:, xoff:xoff + 1024],
                                AF.Identity, scale=st3[:, so + 8:so + 9])
                        em.op("scalar", s_ln2,
                              waits=[("s3b", None), ("s3tr", max(0, tc - 1))], inc="s3ln")

                        def l2tr(e, xoff=xoff):
                            for ec in range(8):
                                col = (0 if ec < 4 else 512) + (ec % 4) * 128
                                inst = e.matmul(
                                    ps3_lnT[:, col:col + 128],
                                    ln2c[:, xoff + ec * 128:xoff + (ec + 1) * 128],
                                    idb3[:], start=(ec % 4 == 0), stop=True)
                            return inst
                        wl = [("s3ln", None), ("s3ev", em.cnt["s3ev"])]
                        if tc == 0:
                            wl.append(("s3ld", base))
                        em.op("tensor", l2tr, waits=wl, inc="s3tr")

                        def l2ev(e, tc=tc):
                            l8 = ln2T.ap().rearrange("p (c n) -> p c n", c=8)
                            e.tensor_copy(l8[:, 0:4, tc * 128:(tc + 1) * 128],
                                          ps3_lnT[:, 0:512].rearrange("p (c n) -> p c n", c=4))
                            return e.tensor_copy(
                                l8[:, 4:8, tc * 128:(tc + 1) * 128],
                                ps3_lnT[:, 512:1024].rearrange("p (c n) -> p c n", c=4))
                        em.op("vector", l2ev, waits=[("s3tr", None)], inc="s3ev")

                        def lgmm(e, tc=tc):
                            for ec in range(8):
                                inst = e.matmul(
                                    ps3_lg[:],
                                    ln2T[:, ec * 2048 + tc * 128:ec * 2048 + (tc + 1) * 128],
                                    gw_sb[:, ec * 8:(ec + 1) * 8],
                                    start=(ec == 0), stop=(ec == 7))
                            return inst
                        em.op("tensor", lgmm,
                              waits=[("s3ev", None), ("s3lgev", em.cnt["s3lgev"])],
                              inc="s3lg")
                        em.op("scalar", lambda e, xb=xb: e.copy(
                            lgsb[:, xb * 8:(xb + 1) * 8], ps3_lg[:]),
                            waits=[("s3lg", None)], inc="s3lgev")
                        em.op("vector", lambda e, xb=xb, so=so: e.reduce_max(
                            st3[:, so + 12:so + 13], lgsb[:, xb * 8:(xb + 1) * 8],
                            axis=AX, negate=True),
                            waits=[("s3lgev", None)], inc="s3sm")
                        em.op("scalar", lambda e, xb=xb, so=so: e.activation(
                            msb[:, xb * 8:(xb + 1) * 8], lgsb[:, xb * 8:(xb + 1) * 8],
                            AF.Exp, bias=st3[:, so + 12:so + 13]),
                            waits=[("s3sm", None), ("s3mt", max(0, tc - 1))],
                            inc="s3ex")

                        p8c = lambda xb=xb: msb[:, xb * 8:(xb + 1) * 8]
                        em.op("vector", lambda e, xb=xb, so=so: e.reduce_sum(
                            st3[:, so + 13:so + 14], p8c(xb), axis=AX),
                            waits=[("s3ex", None)], inc="s3tp")
                        em.op("vector", lambda e, so=so: e.reciprocal(
                            st3[:, so + 14:so + 15], st3[:, so + 13:so + 14]),
                            waits=[("s3tp", None)], inc="s3tp")
                        em.op("vector", lambda e, xb=xb, so=so: e.tensor_scalar_mul(
                            p8c(xb), p8c(xb), st3[:, so + 14:so + 15]),
                            waits=[("s3tp", None)], inc="s3tp")
                        em.op("vector", lambda e, xb=xb, so=so: e.reduce_max(
                            st3[:, so + 13:so + 14], p8c(xb), axis=AX),
                            waits=[("s3tp", None)], inc="s3tp")
                        em.op("vector", lambda e, xb=xb, so=so: e.tensor_single_scalar(
                            mwork[:], p8c(xb), st3[:, so + 13:so + 14], ALU.is_ge),
                            waits=[("s3tp", None)], inc="s3tp")
                        em.op("vector", lambda e, xb=xb: e.scalar_tensor_tensor(
                            mwork[:], mwork[:], -1e30, p8c(xb), ALU.mult, ALU.add),
                            waits=[("s3tp", None)], inc="s3tp")
                        em.op("vector", lambda e, so=so: e.reduce_max(
                            st3[:, so + 13:so + 14], mwork[:], axis=AX),
                            waits=[("s3tp", None)], inc="s3tp")
                        em.op("vector", lambda e, xb=xb, so=so: e.tensor_single_scalar(
                            mwork[:], p8c(xb), st3[:, so + 13:so + 14], ALU.is_ge),
                            waits=[("s3tp", None)], inc="s3tp")
                        em.op("vector", lambda e, xb=xb: e.tensor_tensor(
                            p8c(xb), p8c(xb), mwork[:], ALU.mult),
                            waits=[("s3tp", None)], inc="s3m")
                        def mtr(e, xb=xb):
                            for ex in range(8):
                                inst = e.matmul(
                                    ps3_mt[0:1, ex * 128:(ex + 1) * 128],
                                    msb[:, xb * 8 + ex:xb * 8 + ex + 1], idf3[:],
                                    start=(ex == 0 or ex == 4), stop=True)
                            return inst
                        em.op("tensor", mtr,
                              waits=[("s3m", None), ("s3mtev", em.cnt["s3mtev"])],
                              inc="s3mt")
                        em.op("vector", lambda e, tc=tc: e.tensor_copy(
                            mT_sb.ap().rearrange("p (a n) -> p a n", a=8)[:, :, tc * 128:(tc + 1) * 128],
                            ps3_mt.ap().rearrange("p (a n) -> p a n", a=8)),
                            waits=[("s3mt", None)], inc="s3mtev")
                    em.wait("gpsimd", "s3xw", 16 * 16)
                    em.wait("vector", "s3mtev", 16)

                for which in ENGS:
                    getattr(blk3, which)(
                        (lambda w: lambda h: phase3(Em(w, h, sems3)))(which))

        nc.all_engine_barrier()

        if phases < 4:
            _emit_dumps(nc, dump, dict(KS=KS, QS=QS, VS=VS, GS=GS, PP=PP, PR=PR, X2S=X2S, LNS=LNS, VS2=VS2, HS2=HS2, OS2=OS2))
            return nc
        with contextlib.ExitStack() as _st:
            ones_f = _st.enter_context(nc.sbuf_tensor("ones_f", [1, 128], BF16))
            eb_sb = _st.enter_context(nc.sbuf_tensor("eb_sb", [128, 64], F32))
            ew_sb = _st.enter_context(nc.sbuf_tensor("ew_sb", [128, 2 * 8192], BF16))
            silu_sb = _st.enter_context(nc.sbuf_tensor("silu_sb", [128, 2 * 512], BF16))
            mbc_sb = _st.enter_context(nc.sbuf_tensor("mbc_sb", [128, 512], F32))
            prod_sb = _st.enter_context(nc.sbuf_tensor("prod_sb", [128, 512], F32))
            ps4_pre = _st.enter_context(nc.psum_tensor("ps4_pre", [128, 2 * 512], F32))
            ps4_mbc = _st.enter_context(nc.psum_tensor("ps4_mbc", [128, 512], F32))
            s4_names = ["s4ld", "s4mbc", "s4mm", "s4si", "s4ac", "s4mf", "s4pr", "s4mb2"]
            if True:
                stack = _st
                sems4 = {n: stack.enter_context(nc.semaphore(n)) for n in s4_names}
                blk4 = stack.enter_context(nc.Block())

                def phase4b(em: Em):
                    em.op("gpsimd", lambda e: e.dma_start(ones_f[:], ones_h.ap()[:]),
                          inc="s4ld", incn=16)
                    em.op("sync", lambda e: e.dma_start(eb_sb[:], ap_in["eb_r"][:]),
                          inc="s4ld", incn=16)
                    em.op("gpsimd", lambda e: e.dma_start(
                        ew_sb[:, 0:8192].rearrange("p (c n) -> p c n", c=8),
                        ap_in["ewT"][0].rearrange("(c p) n -> p c n", p=128)),
                        inc="s4ld", incn=16)
                    for ex in range(NEX):
                        if ex + 1 < NEX:
                            em.op("gpsimd", lambda e, ex=ex: e.dma_start(
                                ew_sb[:, ((ex + 1) % 2) * 8192:(((ex + 1) % 2) + 1) * 8192].rearrange("p (c n) -> p c n", c=8),
                                ap_in["ewT"][ex + 1].rearrange("(c p) n -> p c n", p=128)),
                                waits=[("s4mm", 32 * ex)], inc="s4ld", incn=16)
                        eoff = (ex % 2) * 8192
                        for tg in range(4):
                            em.op("tensor", lambda e, ex=ex, tg=tg: e.matmul(
                                ps4_mbc[:], ones_f[:],
                                mT_sb[0:1, ex * 2048 + tg * 512:ex * 2048 + (tg + 1) * 512],
                                start=True, stop=True),
                                waits=[("s4ld", 48), ("s4mf", None)],
                                inc="s4mbc")
                            em.op("vector", lambda e: e.tensor_copy(mbc_sb[:], ps4_mbc[:]),
                                  waits=[("s4mbc", None)], inc="s4mb2")
                            for dc in range(8):
                                gi = em.cnt["s4mm"]

                                def premm(e, eoff=eoff, tg=tg, dc=dc, gi=gi):
                                    pb = (gi % 2) * 512
                                    for ec in range(8):
                                        inst = e.matmul(
                                            ps4_pre[:, pb:pb + 512],
                                            ew_sb[:, eoff + ec * 1024 + dc * 128:eoff + ec * 1024 + (dc + 1) * 128],
                                            ln2T[:, ec * 2048 + tg * 512:ec * 2048 + (tg + 1) * 512],
                                            start=(ec == 0), stop=(ec == 7))
                                    return inst
                                em.op("tensor", premm,
                                      waits=[("s4si", max(0, gi - 1)),
                                             ("s4ld", 48 + 16 * ex)],
                                      inc="s4mm")
                                wsi = [("s4mm", None), ("s4ac", max(0, gi - 1))]
                                if gi == 0:
                                    wsi.append(("s4ld", 48))
                                em.op("scalar", lambda e, ex=ex, dc=dc, gi=gi: e.activation(
                                    silu_sb[:, (gi % 2) * 512:((gi % 2) + 1) * 512],
                                    ps4_pre[:, (gi % 2) * 512:((gi % 2) + 1) * 512],
                                    AF.Silu, bias=eb_sb[:, ex * 8 + dc:ex * 8 + dc + 1]),
                                    waits=wsi, inc="s4si")

                                if ex == 0:
                                    em.op("vector", lambda e, tg=tg, dc=dc, gi=gi: e.tensor_tensor(
                                        acc[:, dc * 2048 + tg * 512:dc * 2048 + (tg + 1) * 512],
                                        silu_sb[:, (gi % 2) * 512:((gi % 2) + 1) * 512],
                                        mbc_sb[:], ALU.mult),
                                        waits=[("s4si", None), ("s4mb2", None)], inc="s4ac")
                                else:
                                    em.op("vector", lambda e, gi=gi: e.tensor_tensor(
                                        prod_sb[:], silu_sb[:, (gi % 2) * 512:((gi % 2) + 1) * 512],
                                        mbc_sb[:], ALU.mult),
                                        waits=[("s4si", None), ("s4mb2", None)], inc="s4pr")
                                    em.op("vector", lambda e, tg=tg, dc=dc: e.tensor_tensor(
                                        acc[:, dc * 2048 + tg * 512:dc * 2048 + (tg + 1) * 512],
                                        acc[:, dc * 2048 + tg * 512:dc * 2048 + (tg + 1) * 512],
                                        prod_sb[:], ALU.add),
                                        waits=[("s4pr", None)], inc="s4ac")
                            em.op("vector", lambda e: e.nop(), inc="s4mf")
                    em.wait("vector", "s4ac", 256)

                for which in ENGS:
                    getattr(blk4, which)(
                        (lambda w: lambda h: phase4b(Em(w, h, sems4)))(which))

        _ln2t_stack.close()
        nc.all_engine_barrier()
        if phases < 5:
            _emit_dumps(nc, dump, dict(KS=KS, QS=QS, VS=VS, GS=GS, PP=PP, PR=PR, X2S=X2S, LNS=LNS, VS2=VS2, HS2=HS2, OS2=OS2))
            return nc

        with contextlib.ExitStack() as _st:
            accb = _st.enter_context(nc.sbuf_tensor("accb", [128, 16384], BF16))
            cw_sb = _st.enter_context(nc.sbuf_tensor("cw_sb", [128, 8192], BF16))
            x2r = _st.enter_context(nc.sbuf_tensor("x2r", [128, 2 * 1024], F32))
            yst = _st.enter_context(nc.sbuf_tensor("yst", [128, 2 * 1024], F32))
            ps5 = _st.enter_context(nc.psum_tensor("ps5", [128, 2 * 512], F32))
            s5_names = ["s5ld", "s5cast", "s5mm", "s5ev", "s5out"]
            if True:
                stack = _st
                sems5 = {n: stack.enter_context(nc.semaphore(n)) for n in s5_names}
                blk5 = stack.enter_context(nc.Block())

                def phase4c(em: Em):
                    em.op("gpsimd", lambda e: e.dma_start(
                        cw_sb.ap().rearrange("p (c n) -> p c n", c=8),
                        ap_in["cwT"].rearrange("(c p) n -> p c n", p=128)),
                        inc="s5ld", incn=16)
                    x2v = X2S.rearrange("(c p) e -> p c e", p=128)
                    for t0 in range(2):
                        em.op("sync", lambda e, t0=t0: e.dma_start(
                            x2r[:, t0 * 1024:(t0 + 1) * 1024], x2v[:, t0, :]),
                            inc="s5ld", incn=16)
                    for i in range(16):
                        em.op("vector", lambda e, i=i: e.tensor_copy(
                            accb[:, i * 1024:(i + 1) * 1024],
                            acc[:, i * 1024:(i + 1) * 1024]),
                            inc="s5cast")
                    for tc in range(16):
                        xb = tc % 2
                        if tc + 2 < 16:
                            em.op("sync", lambda e, tc=tc, xb=xb: e.dma_start(
                                x2r[:, xb * 1024:(xb + 1) * 1024], x2v[:, tc + 2, :]),
                                waits=[("s5ev", 2 * (tc + 1))], inc="s5ld", incn=16)
                        for half in range(2):
                            p = em.cnt["s5mm"]

                            def cbmm(e, tc=tc, half=half, p=p):
                                pb = (p % 2) * 512
                                for dmc in range(8):
                                    inst = e.matmul(
                                        ps5[:, pb:pb + 512],
                                        accb[:, dmc * 2048 + tc * 128:dmc * 2048 + (tc + 1) * 128],
                                        cw_sb[:, dmc * 1024 + half * 512:dmc * 1024 + (half + 1) * 512],
                                        start=(dmc == 0), stop=(dmc == 7))
                                return inst
                            em.op("tensor", cbmm,
                                  waits=[("s5cast", 16), ("s5ld", 48),
                                         ("s5ev", max(0, p - 1))],
                                  inc="s5mm")
                            em.op("vector", lambda e, half=half, p=p, xb=xb, tc=tc: e.tensor_tensor(
                                yst[:, xb * 1024 + half * 512:xb * 1024 + (half + 1) * 512],
                                ps5[:, (p % 2) * 512:((p % 2) + 1) * 512],
                                x2r[:, xb * 1024 + half * 512:xb * 1024 + (half + 1) * 512],
                                ALU.add),
                                waits=[("s5mm", None), ("s5ld", 32 + 16 * tc),
                                       ("s5out", 16 * max(0, tc - 1))],
                                inc="s5ev")
                        em.op("gpsimd", lambda e, tc=tc, xb=xb: e.dma_start(
                            y.rearrange("(c p) e -> p c e", p=128)[:, tc, :],
                            yst[:, xb * 1024:(xb + 1) * 1024]),
                            waits=[("s5ev", None)], inc="s5out", incn=16)
                    em.wait("gpsimd", "s5out", 16 * 16)

                for which in ENGS:
                    getattr(blk5, which)(
                        (lambda w: lambda h: phase4c(Em(w, h, sems5)))(which))

    nc.all_engine_barrier()
    _emit_dumps(nc, dump, dict(KS=KS, QS=QS, VS=VS, GS=GS, PP=PP, PR=PR, X2S=X2S, LNS=LNS, VS2=VS2, HS2=HS2, OS2=OS2))
    return nc


def _emit_dumps(nc, dump, scratch):
    if dump:
        nc.all_engine_barrier()
    for nm in dump:
        t = scratch[nm]
        shp = list(t.tensor.shape)
        dt = t.tensor.dtype
        out = nc.dram_tensor("dbg_" + nm, shp, dt, kind="ExternalOutput").ap()
        nchk = shp[0] // 128
        with (
            nc.sbuf_tensor("dmp_" + nm, [128, shp[1]], dt) as dmp,
            nc.semaphore("sd_" + nm) as sd,
            nc.Block() as blkd,
        ):
            def cpy(e, t=t, out=out, dmp=dmp, shp=shp, sd=sd, nchk=nchk):
                tv = t.rearrange("(c p) n -> p c n", p=128)
                ov = out.rearrange("(c p) n -> p c n", p=128)
                for c in range(nchk):
                    e.dma_start(dmp[:], tv[:, c, :]).then_inc(sd, 16)
                    e.wait_ge(sd, 32 * c + 16)
                    e.dma_start(ov[:, c, :], dmp[:]).then_inc(sd, 16)
                    e.wait_ge(sd, 32 * c + 32)
            blkd.sync(cpy)


_NC_CACHE = {}


def kernel(L0_attn_hidden_states, L0_attn_w_qkv, L0_attn_b_qkv, L0_attn_w_g,
           L0_attn_b_g, L0_attn_w_p, L0_attn_b_p, L0_ffn_gate_w,
           L0_ffn_expert_w, L0_ffn_expert_b, L0_ffn_combine_w, _dump=()):
    x = np.asarray(L0_attn_hidden_states, dtype=np.float32)
    args = [np.asarray(a, dtype=np.float32) for a in
            (L0_attn_w_qkv, L0_attn_b_qkv, L0_attn_w_g, L0_attn_b_g,
             L0_attn_w_p, L0_attn_b_p, L0_ffn_gate_w, L0_ffn_expert_w,
             L0_ffn_expert_b, L0_ffn_combine_w)]
    in_maps = [_stage_core(c, x, *args) for c in range(NCORE)]
    key = tuple(sorted(_dump))
    if key not in _NC_CACHE:
        _NC_CACHE[key] = build_nc(dump=_dump)
    res = run_bass_kernel_spmd(_NC_CACHE[key], in_maps, core_ids=list(range(NCORE)))
    out = np.empty((B, T, E), dtype=np.float32)
    for c in range(NCORE):
        b, q = c // 4, c % 4
        out[b, q * TOWN:(q + 1) * TOWN] = res.results[c]["y"]
    if _dump:
        return out, [{k: np.asarray(v) for k, v in r.items() if k.startswith("dbg_")}
                     for r in res.results]
    return out
